# version 23
# baseline (speedup 1.0000x reference)
import numpy as np
import ml_dtypes
from contextlib import ExitStack

import concourse.bass as bass
import concourse.mybir as mybir
from concourse.bass_utils import run_bass_kernel_spmd

F32 = mybir.dt.float32
BF16 = mybir.dt.bfloat16
AF = mybir.ActivationFunctionType
ALU = mybir.AluOpType
AX = mybir.AxisListType
bfloat16 = ml_dtypes.bfloat16

D = 1024
H = 16
HD = 64
C = 128
N_CORES = 8

PE_LABELS = ["qg", "vg", "qT", "lnT", "at1", "at2", "at3", "at4",
             "intra", "inter", "g", "kk", "rpre", "kaT", "ln2T", "out3"]
DV_LABELS = ["var1", "qb", "qTev", "nrm1", "msq", "vn1", "msv", "vn2",
             "mke", "mko", "sadd", "y", "var2", "nrm2", "mixd", "xr", "zmul"]
AC_LABELS = ["sumsq", "sqQ", "sqrt1", "ln1n", "sqrtQ", "lnTev", "sqV",
             "sqrtV", "relu", "kab", "ysq", "sqrt2", "ybs", "sig",
             "kaTev", "ln2Tev", "bcopy"]
GP_LABELS = ["qtf", "sbd", "xkb", "z"]

NPE, NDV, NAC, NGP = map(len, (PE_LABELS, DV_LABELS, AC_LABELS, GP_LABELS))
PE_PRO = 1
DV_PRO = 3
AC_PRO = 4


def M_pe(lbl, i): return PE_PRO + i * NPE + PE_LABELS.index(lbl) + 1
def M_dv(lbl, i): return DV_PRO + i * NDV + DV_LABELS.index(lbl) + 1
def M_ac(lbl, i): return AC_PRO + i * NAC + AC_LABELS.index(lbl) + 1
def M_gp(lbl, i): return i * NGP + GP_LABELS.index(lbl) + 1


def C_ln1n(i):
    return 3 if i == 0 else M_ac("ln1n", i - 1)


def C_lnTev(i):
    return AC_PRO if i == 0 else M_ac("lnTev", i - 1)


def build_program(n_chunks, dbg=False):
    N = n_chunks
    ROWS = N * C
    nc = bass.Bass("TRN2", target_bir_lowering=False, debug=False,
                   detect_race_conditions=False)

    xd = nc.dram_tensor("x", [ROWS, D], F32, kind="ExternalInput").ap()
    zd = nc.dram_tensor("z", [ROWS, D], F32, kind="ExternalOutput").ap()
    w_d = {n: nc.dram_tensor(n, [D, D], F32, kind="ExternalInput").ap()
           for n in ("wqT", "wvT", "wrT", "wkT", "woT")}
    identb_d = nc.dram_tensor("identb", [128, 128], BF16, kind="ExternalInput").ap()
    qwrep_d = nc.dram_tensor("qwrep", [128, D], F32, kind="ExternalInput").ap()
    vwrep_d = nc.dram_tensor("vwrep", [128, D], F32, kind="ExternalInput").ap()
    mask_d = nc.dram_tensor("maskrep", [128, D], BF16, kind="ExternalInput").ap()
    murF_d = nc.dram_tensor("murF", [128, 8], F32, kind="ExternalInput").ap()
    mukF_d = nc.dram_tensor("mukF", [128, 8], F32, kind="ExternalInput").ap()
    consts_d = nc.dram_tensor("consts", [128, 4], F32, kind="ExternalInput").ap()
    N_CONST_DMAS = 7
    dbg_d = {}

    with ExitStack() as ST:
        def sb(name, shape, dtype=F32):
            return ST.enter_context(nc.sbuf_tensor(name, shape, dtype))

        def psum(name, shape, dtype=F32):
            return ST.enter_context(nc.psum_tensor(name, shape, dtype))

        def sem(name):
            return ST.enter_context(nc.semaphore(name))

        xs = [sb("xs0", [128, D]), sb("xs1", [128, D])]
        zs = sb("zs", [128, D])
        ys = [sb("ys0", [128, D]), sb("ys1", [128, D])]
        ln1b = [sb("ln1b0", [128, D], BF16), sb("ln1b1", [128, D], BF16)]
        lnT = [sb("lnT0", [128, D], BF16), sb("lnT1", [128, D], BF16)]
        qb = [sb("qb0", [128, D], BF16), sb("qb1", [128, D], BF16)]
        qTe = sb("qTe", [128, D], BF16)
        qTo = sb("qTo", [128, D], BF16)
        qTf = sb("qTf", [128, D], BF16)
        vn = [sb("vn0", [128, D], BF16), sb("vn1_", [128, D], BF16)]
        amask = sb("am", [128, 2048], BF16)
        ybs = sb("ybs", [128, D], BF16)
        diagr = sb("diagr", [128, 128], BF16)
        ln2T = [sb("ln2T0", [128, 8 * 129], BF16), sb("ln2T1", [128, 8 * 129], BF16)]
        kaT = sb("kaT", [128, D], BF16)
        sbd = [sb("sbd0", [128, D], BF16), sb("sbd1", [128, D], BF16)]
        Ssum = sb("Ssum", [128, D])
        rs = sb("rs", [128, D])
        kab = sb("kab", [128, D], BF16)
        sqB = sb("sqB", [128, D])
        tscr = sb("tscr", [128, D])
        vscr = sb("vscr", [128, D])
        y1scr = sb("y1scr", [128, D])
        w_s = {n: sb(n + "_s", [128, 8 * D], BF16) for n in w_d}
        xrT = sb("xrT", [128, D], BF16)
        xkT = sb("xkT", [128, D], BF16)
        dscr = sb("dscr", [128, D], BF16)
        t1scr = sb("t1scr", [128, D], BF16)
        t1g = sb("t1g", [128, D], BF16)
        murFb = sb("murFb", [128, 8], BF16)
        mukFb = sb("mukFb", [128, 8], BF16)
        identb = sb("identb_s", [128, 128], BF16)
        qwrep = sb("qwrep_s", [128, D])
        vwrep = sb("vwrep_s", [128, D])
        maskrep = sb("maskrep_s", [128, D], BF16)
        murF = sb("murF_s", [128, 8])
        mukF = sb("mukF_s", [128, 8])
        consts = sb("consts_s", [128, 4])
        tin = {n: sb(n, [128, 2]) for n in
               ("s1", "mean1", "m2", "var1", "sqv1", "rstd1", "b1")}
        tin["ss1"] = sb("ss1", [128, 4])
        for a, b in (("s2", "s1"), ("nmean2", "mean1"), ("m22", "m2"),
                     ("var2", "var1"), ("sqv2", "sqv1"), ("rstd2", "rstd1"),
                     ("ss2", "ss1")):
            tin[a] = tin[b]
        hh = {n: sb(n, [128, 32]) for n in
              ("ms", "sqmsQ", "sqmsV", "rsfq", "rsv")}
        hh["msq"] = hh["msv"] = hh["ms"]

        ps_s = psum("ps_s", [128, D])
        ps_p1 = psum("ps_p1", [128, D])
        ps_p2 = psum("ps_p2", [128, D])
        ps_pb = psum("ps_pb", [128, D], BF16)
        ps_pb2 = psum("ps_pb2", [128, D], BF16)

        s_x = [sem("s_x0"), sem("s_x1")]
        s_w = sem("s_w"); s_cst = sem("s_cst")
        s_z = sem("s_z")
        s_pe = sem("s_pe"); s_dv = sem("s_dv"); s_ac = sem("s_ac"); s_gp = sem("s_gp")
        s_fd = sem("s_fd")
        fence_ct = [0]

        def fence(dv_eng, ins):
            fence_ct[0] += 1
            ins.then_inc(s_fd, 1)
            dv_eng.wait_ge(s_fd, fence_ct[0])

        hw = {}

        def wt(eng, key, s, v):
            if v <= 0:
                return
            k = (key, id(s))
            if hw.get(k, 0) >= v:
                return
            hw[k] = v
            eng.wait_ge(s, v)

        emitted = {"pe": 0, "dv": 0, "ac": 0, "gp": 0}

        def mark(key, instr, sem_h, val):
            assert val == emitted[key] + 1, (key, val, emitted[key])
            emitted[key] = val
            instr.then_inc(sem_h, 1)

        def xcount(c):
            return 16 * (c // 2 + 1)

        block = ST.enter_context(nc.Block())

        @block.sync
        def _(sync):
            for ap_s, ap_d in ((identb, identb_d),
                               (qwrep, qwrep_d), (vwrep, vwrep_d),
                               (maskrep, mask_d), (murF, murF_d),
                               (mukF, mukF_d), (consts, consts_d)):
                sync.dma_start(out=ap_s[:], in_=ap_d).then_inc(s_cst, 16)
            sync.dma_start(out=xs[0][:], in_=xd[0:C, :]).then_inc(s_x[0], 16)
            if N > 1:
                sync.dma_start(out=xs[1][:], in_=xd[C:2 * C, :]).then_inc(s_x[1], 16)
            for i in range(N):
                if i + 2 < N:
                    wt(sync, "sy", s_dv, M_dv("y", i))
                    wt(sync, "sy", s_ac, C_ln1n(i))
                    sync.dma_start(out=xs[i % 2][:],
                                   in_=xd[(i + 2) * C:(i + 3) * C, :]).then_inc(s_x[i % 2], 16)
                if i >= 1:
                    wt(sync, "sy", s_gp, M_gp("z", i))
                    sync.dma_start(out=zd[(i - 1) * C:i * C, :],
                                   in_=zs[:]).then_inc(s_z, 16)
            wt(sync, "sy", s_gp, M_gp("z", N - 1) + 1)
            sync.dma_start(out=zd[(N - 1) * C:N * C, :],
                           in_=zs[:]).then_inc(s_z, 16)
            sync.wait_ge(s_z, 16 * N)
            if dbg:
                q_ = (N - 1) % 2
                dbg_map = {"ln1b": ln1b[q_], "lnT": lnT[q_], "qb": qb[q_],
                           "qTe": qTe, "qTo": qTo, "qTf": qTf, "vn": vn[q_],
                           "amask": amask,
                           "ys0": ys[0][:], "ys1": ys[1][:], "Ssum": Ssum,
                           "sbd0": sbd[0][:], "sbd1": sbd[1][:], "rs": rs,
                           "kab": kab, "kaT": kaT, "ybs": ybs,
                           "ln2T0": ln2T[0][:], "ln2T1": ln2T[1][:],
                           "tscr": tscr, "y1scr": y1scr, "sqB": sqB}
                sync.wait_ge(s_dv, M_dv("zmul", N - 1) + 1)
                sync.wait_ge(s_ac, M_ac("bcopy", N - 1) + 4)
                sync.wait_ge(s_gp, M_gp("z", N - 1) + 1)
                nd = 0
                for nm, t in dbg_map.items():
                    ap = t[:] if hasattr(t, "shape") and not isinstance(t, bass.AP) else t
                    dd = nc.dram_tensor("dbg_" + nm, list(ap.shape), ap.dtype,
                                        kind="ExternalOutput").ap()
                    dbg_d[nm] = dd
                    sync.dma_start(out=dd, in_=ap).then_inc(s_cst, 16)
                    nd += 1
                sync.wait_ge(s_cst, 16 * (N_CONST_DMAS + nd))

        @block.gpsimd
        def _(gp):
            for n in ("wqT", "wvT", "wrT", "wkT", "woT"):
                for ct in range(8):
                    gp.dma_start(out=w_s[n][:, ct * D:(ct + 1) * D],
                                 in_=w_d[n][ct * 128:(ct + 1) * 128, :]
                                 ).then_inc(s_w, 16)
            for i in range(N):
                wt(gp, "gp", s_cst, 16 * N_CONST_DMAS)
                wt(gp, "gp", s_dv, 1)
                wt(gp, "gp", s_dv, M_dv("qTev", i))
                wt(gp, "gp", s_pe, M_pe("inter", i - 1))
                ins = gp.tensor_add(qTf[:], qTe[:], qTo[:])
                mark("gp", ins, s_gp, M_gp("qtf", i))
                wt(gp, "gp", s_dv, M_dv("sadd", i))
                sB = Ssum[0:64, :].rearrange("p (ct c) -> p ct c", ct=8)
                dB = sbd[(i + 1) % 2][0:64, :].rearrange("p (ct c) -> p ct c", ct=8)
                gp.tensor_copy(out=dB[:, :, 0:64], in_=sB[:, :, 0:64])
                sB = Ssum[64:128, :].rearrange("p (ct c) -> p ct c", ct=8)
                dB = sbd[(i + 1) % 2][64:128, :].rearrange("p (ct c) -> p ct c", ct=8)
                ins = gp.tensor_copy(out=dB[:, :, 64:128], in_=sB[:, :, 64:128])
                mark("gp", ins, s_gp, M_gp("sbd", i))
                wt(gp, "gp", s_dv, M_dv("mixd", i))
                wt(gp, "gp", s_pe, M_pe("kk", i))
                lt2 = ln2T[i % 2][:].rearrange("p (a b) -> p a b", a=8)
                xpv = lt2[:, :, 0:128]
                d3 = dscr[:].rearrange("p (a b) -> p a b", a=8)
                t13g = t1g[:].rearrange("p (a b) -> p a b", a=8)
                mukb = mukFb[:].unsqueeze(2).broadcast_to((128, 8, C))
                gp.tensor_tensor(out=t13g, in0=d3, in1=mukb, op=ALU.mult)
                ins = gp.tensor_tensor(out=xkT[:].rearrange("p (a b) -> p a b", a=8),
                                       in0=xpv, in1=t13g, op=ALU.add)
                mark("gp", ins, s_gp, M_gp("xkb", i))
                wt(gp, "gp", s_dv, M_dv("zmul", i))
                wt(gp, "gpz", s_z, 16 * (i - 1))
                ins = gp.tensor_tensor(out=zs[:], in0=y1scr[:],
                                       in1=ys[(i - 1) % 2][:], op=ALU.add)
                mark("gp", ins, s_gp, M_gp("z", i))
            gp.wait_ge(s_dv, M_dv("zmul", N - 1) + 1)
            wt(gp, "gpz", s_z, 16 * (N - 1))
            ins = gp.tensor_tensor(out=zs[:], in0=y1scr[:],
                                   in1=ys[(N - 1) % 2][:], op=ALU.add)
            ins.then_inc(s_gp, 1)

        @block.scalar
        def _(ac):
            def ln1_next(c):
                pc = c % 2
                if c < N:
                    wt(ac, "ac%d" % pc, s_x[pc], xcount(c))
                src = xs[pc]
                ac.activation(out=tscr[:, 0:512], in_=src[:, 0:512],
                              func=AF.Square,
                              accum_out=tin["ss1"][:, 2 * pc:2 * pc + 1])
                ins = ac.activation(out=tscr[:, 512:1024], in_=src[:, 512:1024],
                                    func=AF.Square,
                                    accum_out=tin["ss1"][:, 2 * pc + 1:2 * pc + 2])
                yield ins
                ins = ac.activation(out=tin["sqv1"][:, pc:pc + 1],
                                    in_=tin["var1"][:, pc:pc + 1], func=AF.Sqrt,
                                    bias=consts[:, 0:1])
                yield ins
                ins = ac.activation(out=ln1b[pc][:], in_=src[:], func=AF.Identity,
                                    bias=tin["b1"][:, pc:pc + 1],
                                    scale=tin["rstd1"][:, pc:pc + 1])
                yield ins

            wt(ac, "ac", s_cst, 16 * N_CONST_DMAS)
            g0 = ln1_next(0)
            ins = next(g0)
            ins.then_inc(s_ac, 1)
            wt(ac, "ac", s_dv, 2)
            ins = next(g0)
            ins.then_inc(s_ac, 2 - 1)
            wt(ac, "ac", s_dv, 3)
            ins = next(g0)
            ins.then_inc(s_ac, 1)
            wt(ac, "ac", s_pe, PE_PRO)
            ins = ac.copy(out=lnT[0][:], in_=ps_pb2[:])
            ins.then_inc(s_ac, 1)
            emitted["ac"] = AC_PRO

            for i in range(N):
                p = i % 2
                g1 = ln1_next(i + 1)
                ins = next(g1)
                mark("ac", ins, s_ac, M_ac("sumsq", i))
                wt(ac, "ac", s_pe, M_pe("qg", i))
                ins = ac.activation(out=sqB[:], in_=ps_p1[:], func=AF.Square)
                mark("ac", ins, s_ac, M_ac("sqQ", i))
                wt(ac, "ac", s_dv, M_dv("var1", i))
                ins = next(g1)
                mark("ac", ins, s_ac, M_ac("sqrt1", i))
                wt(ac, "ac", s_dv, M_dv("nrm1", i))
                wt(ac, "ac", s_pe, M_pe("g", i - 1))
                ins = next(g1)
                mark("ac", ins, s_ac, M_ac("ln1n", i))
                wt(ac, "ac", s_dv, M_dv("msq", i))
                ins = ac.activation(out=hh["sqmsQ"][:, p * 16:(p + 1) * 16],
                                    in_=hh["msq"][:, p * 16:(p + 1) * 16],
                                    func=AF.Sqrt, bias=consts[:, 1:2],
                                    scale=consts[:, 2:3])
                mark("ac", ins, s_ac, M_ac("sqrtQ", i))
                wt(ac, "ac", s_pe, M_pe("lnT", i))
                wt(ac, "ac", s_pe, M_pe("at4", i - 1))
                ins = ac.copy(out=lnT[(i + 1) % 2][:], in_=ps_pb2[:])
                mark("ac", ins, s_ac, M_ac("lnTev", i))
                wt(ac, "ac", s_pe, M_pe("vg", i))
                ins = ac.activation(out=sqB[:], in_=ps_p2[:], func=AF.Square)
                mark("ac", ins, s_ac, M_ac("sqV", i))
                wt(ac, "ac", s_dv, M_dv("msv", i))
                ins = ac.activation(out=hh["sqmsV"][:, p * 16:(p + 1) * 16],
                                    in_=hh["msv"][:, p * 16:(p + 1) * 16],
                                    func=AF.Sqrt, bias=consts[:, 1:2],
                                    scale=consts[:, 2:3])
                mark("ac", ins, s_ac, M_ac("sqrtV", i))
                wt(ac, "ac", s_pe, M_pe("kk", i))
                ins = ac.activation(out=tscr[:], in_=ps_p2[:], func=AF.Relu)
                mark("ac", ins, s_ac, M_ac("relu", i))
                wt(ac, "ac", s_pe, M_pe("kaT", i - 1))
                ins = ac.activation(out=kab[:], in_=tscr[:], func=AF.Square)
                mark("ac", ins, s_ac, M_ac("kab", i))
                wt(ac, "ac", s_dv, M_dv("y", i))
                ac.activation(out=tscr[:, 0:512], in_=ys[p][:, 0:512],
                              func=AF.Square,
                              accum_out=tin["ss2"][:, 2 * p:2 * p + 1])
                ins = ac.activation(out=tscr[:, 512:1024], in_=ys[p][:, 512:1024],
                                    func=AF.Square,
                                    accum_out=tin["ss2"][:, 2 * p + 1:2 * p + 2])
                mark("ac", ins, s_ac, M_ac("ysq", i))
                wt(ac, "ac", s_dv, M_dv("var2", i))
                ins = ac.activation(out=tin["sqv2"][:, p:p + 1],
                                    in_=tin["var2"][:, p:p + 1], func=AF.Sqrt,
                                    bias=consts[:, 0:1])
                mark("ac", ins, s_ac, M_ac("sqrt2", i))
                wt(ac, "ac", s_pe, M_pe("ln2T", i - 1))
                ins = ac.activation(out=ybs[:], in_=ys[p][:], func=AF.Identity,
                                    bias=tin["nmean2"][:, p:p + 1])
                mark("ac", ins, s_ac, M_ac("ybs", i))
                wt(ac, "ac", s_pe, M_pe("rpre", i))
                wt(ac, "ac", s_dv, M_dv("zmul", i - 1))
                ins = ac.activation(out=rs[:], in_=ps_p1[:], func=AF.Sigmoid)
                mark("ac", ins, s_ac, M_ac("sig", i))
                wt(ac, "ac", s_pe, M_pe("kaT", i))
                wt(ac, "ac", s_pe, M_pe("out3", i - 1))
                ins = ac.copy(out=kaT[:], in_=ps_pb[:])
                mark("ac", ins, s_ac, M_ac("kaTev", i))
                wt(ac, "ac", s_pe, M_pe("ln2T", i))
                wt(ac, "ac", s_dv, M_dv("xr", i - 2))
                wt(ac, "ac", s_gp, M_gp("xkb", i - 2))
                dst = ln2T[p][:].rearrange("p (a b) -> p a b", a=8)[:, :, 1:129]
                ins = ac.copy(out=dst, in_=ps_p1[:].rearrange("p (a b) -> p a b", a=8))
                mark("ac", ins, s_ac, M_ac("ln2Tev", i))
                wt(ac, "ac", s_dv, M_dv("xr", i - 1))
                wt(ac, "ac", s_gp, M_gp("xkb", i - 1))
                src = ln2T[p][:].rearrange("p (a b) -> p a b", a=8)[:, :, 128:129]
                dst = ln2T[(i + 1) % 2][:].rearrange("p (a b) -> p a b", a=8)[:, :, 0:1]
                ins = ac.copy(out=dst, in_=src)
                mark("ac", ins, s_ac, M_ac("bcopy", i))
            tot_pe = M_pe("out3", N - 1)
            tot_dv = M_dv("zmul", N - 1)
            ac.wait_ge(s_pe, tot_pe + 1)
            ac.activation(out=tscr[:], in_=ps_p2[:],
                          func=AF.Relu).then_inc(s_ac, 1)
            ac.activation(out=kab[:], in_=tscr[:],
                          func=AF.Square).then_inc(s_ac, 1)
            ac.wait_ge(s_pe, tot_pe + 2)
            ac.wait_ge(s_dv, tot_dv)
            ac.activation(out=rs[:], in_=ps_p1[:],
                          func=AF.Sigmoid).then_inc(s_ac, 1)
            ac.wait_ge(s_pe, tot_pe + 3)
            ac.copy(out=kaT[:], in_=ps_pb[:]).then_inc(s_ac, 1)

        @block.vector
        def _(dv):

            def ln1_stats(c):
                pc = c % 2
                if c < N:
                    wt(dv, "dv%d" % pc, s_x[pc], xcount(c))
                ins = dv.tensor_reduce(out=tin["s1"][:, pc:pc + 1], in_=xs[pc][:],
                                       axis=AX.X, op=ALU.add)
                fence(dv, ins)
                ins = dv.tensor_single_scalar(out=tin["mean1"][:, pc:pc + 1],
                                              in_=tin["s1"][:, pc:pc + 1],
                                              scalar=1.0 / D, op=ALU.mult)
                fence(dv, ins)
                dv.tensor_tensor(out=tin["m2"][:, pc:pc + 1],
                                 in0=tin["mean1"][:, pc:pc + 1],
                                 in1=tin["mean1"][:, pc:pc + 1], op=ALU.mult)
                ins = dv.tensor_add(tin["ss1"][:, 2 * pc:2 * pc + 1],
                                    tin["ss1"][:, 2 * pc:2 * pc + 1],
                                    tin["ss1"][:, 2 * pc + 1:2 * pc + 2])
                fence(dv, ins)
                ins = dv.scalar_tensor_tensor(out=tin["var1"][:, pc:pc + 1],
                                              in0=tin["ss1"][:, 2 * pc:2 * pc + 1],
                                              scalar=1.0 / D,
                                              in1=tin["m2"][:, pc:pc + 1],
                                              op0=ALU.mult, op1=ALU.subtract)
                yield ins
                ins = dv.reciprocal(out=tin["rstd1"][:, pc:pc + 1],
                                    in_=tin["sqv1"][:, pc:pc + 1])
                fence(dv, ins)
                ins = dv.scalar_tensor_tensor(out=tin["b1"][:, pc:pc + 1],
                                              in0=tin["mean1"][:, pc:pc + 1],
                                              scalar=-1.0,
                                              in1=tin["rstd1"][:, pc:pc + 1],
                                              op0=ALU.mult, op1=ALU.mult)
                yield ins

            dv.memset(sbd[0][:], 0.0)
            dv.memset(sbd[1][:], 0.0)
            dv.memset(Ssum[:], 0.0)
            dv.memset(qTe[:], 0.0)
            dv.memset(qTo[:], 0.0)
            dv.memset(ys[1][:], 0.0)
            dv.memset(xrT[:], 0.0)
            dv.memset(xkT[:], 0.0)
            wt(dv, "dv", s_cst, 16 * N_CONST_DMAS)
            ins = dv.memset(
                ln2T[0][:].rearrange("p (a b) -> p a b", a=8)[:, :, 0:1], 0.0)
            ins.then_inc(s_dv, 1)
            g0 = ln1_stats(0)
            wt(dv, "dv", s_ac, 1)
            ins = next(g0)
            ins.then_inc(s_dv, 1)
            wt(dv, "dv", s_ac, 2)
            ins = next(g0)
            ins.then_inc(s_dv, 1)
            dv.tensor_copy(out=murFb[:], in_=murF[:])
            dv.tensor_copy(out=mukFb[:], in_=mukF[:])
            emitted["dv"] = DV_PRO

            for i in range(N):
                p = i % 2
                g1 = ln1_stats(i + 1)
                wt(dv, "dv", s_ac, M_ac("sumsq", i))
                ins = next(g1)
                mark("dv", ins, s_dv, M_dv("var1", i))
                wt(dv, "dv", s_ac, M_ac("sqQ", i))
                wt(dv, "dv", s_pe, M_pe("qT", i - 2))
                ins = dv.tensor_tensor(out=qb[p][:], in0=ps_p1[:], in1=qwrep[:],
                                       op=ALU.mult)
                mark("dv", ins, s_dv, M_dv("qb", i))
                wt(dv, "dv", s_pe, M_pe("qT", i))
                wt(dv, "dv", s_pe, M_pe("at4", i - 1))
                wt(dv, "dv", s_gp, M_gp("qtf", i - 1))
                dv.tensor_copy(out=qTe[0:64, :], in_=ps_pb[0:64, :])
                ins = dv.tensor_copy(out=qTo[64:128, :], in_=ps_pb[64:128, :])
                mark("dv", ins, s_dv, M_dv("qTev", i))
                wt(dv, "dv", s_ac, M_ac("sqrt1", i))
                ins = next(g1)
                mark("dv", ins, s_dv, M_dv("nrm1", i))
                ins = dv.tensor_reduce(out=hh["msq"][:, p * 16:(p + 1) * 16],
                                       in_=sqB[:].rearrange("p (h e) -> p h e", h=16),
                                       axis=AX.X, op=ALU.add)
                mark("dv", ins, s_dv, M_dv("msq", i))
                wt(dv, "dv", s_pe, M_pe("vg", i))
                ins = dv.tensor_tensor(out=vscr[:], in0=ps_p2[:], in1=vwrep[:],
                                       op=ALU.mult)
                mark("dv", ins, s_dv, M_dv("vn1", i))
                wt(dv, "dv", s_ac, M_ac("sqV", i))
                ins = dv.tensor_reduce(out=hh["msv"][:, p * 16:(p + 1) * 16],
                                       in_=sqB[:].rearrange("p (h e) -> p h e", h=16),
                                       axis=AX.X, op=ALU.add)
                mark("dv", ins, s_dv, M_dv("msv", i))
                wt(dv, "dv", s_ac, M_ac("sqrtV", i))
                ins = dv.reciprocal(out=hh["rsv"][:, p * 16:(p + 1) * 16],
                                    in_=hh["sqmsV"][:, p * 16:(p + 1) * 16])
                fence(dv, ins)
                wt(dv, "dv", s_pe, M_pe("g", i - 2))
                rsvb = hh["rsv"][:, p * 16:(p + 1) * 16].unsqueeze(2) \
                    .broadcast_to((128, 16, 64))
                ins = dv.tensor_tensor(out=vn[p][:].rearrange("p (h e) -> p h e", h=16),
                                       in0=vscr[:].rearrange("p (h e) -> p h e", h=16),
                                       in1=rsvb, op=ALU.mult)
                mark("dv", ins, s_dv, M_dv("vn2", i))
                wt(dv, "dv", s_pe, M_pe("intra", i - 1))
                wt(dv, "dv", s_pe, M_pe("at2", i))
                ins = dv.tensor_tensor(out=amask[:, 0:1024], in0=ps_p1[:],
                                       in1=maskrep[:], op=ALU.mult)
                mark("dv", ins, s_dv, M_dv("mke", i))
                wt(dv, "dv", s_pe, M_pe("at4", i))
                ins = dv.tensor_tensor(out=amask[:, 1024:2048], in0=ps_p2[:],
                                       in1=maskrep[:], op=ALU.mult)
                mark("dv", ins, s_dv, M_dv("mko", i))
                wt(dv, "dv", s_pe, M_pe("g", i))
                wt(dv, "dv", s_gp, M_gp("sbd", i - 1))
                ins = dv.tensor_tensor(out=Ssum[:], in0=Ssum[:], in1=ps_s[:],
                                       op=ALU.add)
                mark("dv", ins, s_dv, M_dv("sadd", i))
                wt(dv, "dv", s_ac, M_ac("sqrtQ", i))
                ins = dv.reciprocal(out=hh["rsfq"][:, p * 16:(p + 1) * 16],
                                    in_=hh["sqmsQ"][:, p * 16:(p + 1) * 16])
                fence(dv, ins)
                wt(dv, "dv", s_pe, M_pe("inter", i))
                wt(dv, "dv", s_gp, M_gp("z", i - 1))
                rsfb = hh["rsfq"][:, p * 16:(p + 1) * 16].unsqueeze(2) \
                    .broadcast_to((128, 16, 64))
                dv.tensor_tensor(out=y1scr[:].rearrange("p (h e) -> p h e", h=16),
                                 in0=ps_p1[:].rearrange("p (h e) -> p h e", h=16),
                                 in1=rsfb, op=ALU.mult)
                ins = dv.scalar_tensor_tensor(out=ys[p][:], in0=y1scr[:],
                                              scalar=1.0, in1=xs[p][:],
                                              op0=ALU.mult, op1=ALU.add,
                                              accum_out=tin["s2"][:, p:p + 1])
                mark("dv", ins, s_dv, M_dv("y", i))
                ins = dv.tensor_single_scalar(out=tin["nmean2"][:, p:p + 1],
                                              in_=tin["s2"][:, p:p + 1],
                                              scalar=-1.0 / D, op=ALU.mult)
                ins = dv.scalar_tensor_tensor(out=tin["m22"][:, p:p + 1],
                                              in0=tin["s2"][:, p:p + 1],
                                              scalar=1.0 / (D * D),
                                              in1=tin["s2"][:, p:p + 1],
                                              op0=ALU.mult, op1=ALU.mult)
                fence(dv, ins)
                wt(dv, "dv", s_ac, M_ac("ysq", i))
                ins = dv.tensor_add(tin["ss2"][:, 2 * p:2 * p + 1],
                                    tin["ss2"][:, 2 * p:2 * p + 1],
                                    tin["ss2"][:, 2 * p + 1:2 * p + 2])
                fence(dv, ins)
                ins = dv.scalar_tensor_tensor(out=tin["var2"][:, p:p + 1],
                                              in0=tin["ss2"][:, 2 * p:2 * p + 1],
                                              scalar=1.0 / D,
                                              in1=tin["m22"][:, p:p + 1],
                                              op0=ALU.mult, op1=ALU.subtract)
                mark("dv", ins, s_dv, M_dv("var2", i))
                wt(dv, "dv", s_ac, M_ac("sqrt2", i))
                ins = dv.reciprocal(out=tin["rstd2"][:, p:p + 1],
                                    in_=tin["sqv2"][:, p:p + 1])
                fence(dv, ins)
                ins = dv.tensor_single_scalar(out=diagr[:], in_=identb[:],
                                              scalar=tin["rstd2"][:, p:p + 1],
                                              op=ALU.mult)
                mark("dv", ins, s_dv, M_dv("nrm2", i))
                wt(dv, "dv", s_ac, M_ac("ln2Tev", i))
                wt(dv, "dv", s_gp, M_gp("xkb", i - 1))
                lt2 = ln2T[p][:].rearrange("p (a b) -> p a b", a=8)
                xv3 = lt2[:, :, 1:129]
                xpv = lt2[:, :, 0:128]
                d3 = dscr[:].rearrange("p (a b) -> p a b", a=8)
                t13 = t1scr[:].rearrange("p (a b) -> p a b", a=8)
                murb = murFb[:].unsqueeze(2).broadcast_to((128, 8, C))
                ins = dv.tensor_tensor(out=d3, in0=xv3, in1=xpv, op=ALU.subtract)
                mark("dv", ins, s_dv, M_dv("mixd", i))
                dv.tensor_tensor(out=t13, in0=d3, in1=murb, op=ALU.mult)
                wt(dv, "dv", s_pe, M_pe("rpre", i))
                ins = dv.tensor_tensor(out=xrT[:].rearrange("p (a b) -> p a b", a=8),
                                       in0=xpv, in1=t13, op=ALU.add)
                mark("dv", ins, s_dv, M_dv("xr", i))
                wt(dv, "dv", s_pe, M_pe("out3", i))
                wt(dv, "dv", s_ac, M_ac("sig", i))
                ins = dv.tensor_tensor(out=y1scr[:], in0=ps_s[:], in1=rs[:],
                                       op=ALU.mult)
                mark("dv", ins, s_dv, M_dv("zmul", i))
            dv.wait_ge(s_pe, M_pe("out3", N - 1) + 4)
            dv.wait_ge(s_ac, M_ac("bcopy", N - 1) + 3)
            wt(dv, "dv", s_gp, M_gp("z", N - 1))
            ins = dv.tensor_tensor(out=y1scr[:], in0=ps_s[:], in1=rs[:],
                                   op=ALU.mult)
            ins.then_inc(s_dv, 1)

        @block.tensor
        def _(pe):
            wt(pe, "pe", s_cst, 16 * N_CONST_DMAS)
            wt(pe, "pe", s_ac, 3)
            for j in range(8):
                ins = pe.matmul(ps_pb2[:, j * 128:(j + 1) * 128],
                                ln1b[0][:, j * 128:(j + 1) * 128], identb[:],
                                start=(j == 0), stop=(j == 7), is_transpose=True)
            ins.then_inc(s_pe, 1)
            emitted["pe"] = PE_PRO
            for i in range(N):
                p = i % 2
                wt(pe, "pe", s_ac, C_lnTev(i))
                wt(pe, "pe", s_ac, M_ac("ln2Tev", i - 1))
                wt(pe, "pe", s_w, 16 * 8)
                for ct in range(8):
                    lh = lnT[p][:, ct * 128:(ct + 1) * 128]
                    st_, sp = (ct == 0), (ct == 7)
                    pe.matmul(ps_p1[:, 0:512], lh,
                              w_s["wqT"][:, ct * D:ct * D + 512], start=st_, stop=sp)
                    ins = pe.matmul(ps_p1[:, 512:1024], lh,
                                    w_s["wqT"][:, ct * D + 512:ct * D + 1024],
                                    start=st_, stop=sp)
                mark("pe", ins, s_pe, M_pe("qg", i))
                wt(pe, "pe", s_w, 16 * 16)
                for ct in range(8):
                    lh = lnT[p][:, ct * 128:(ct + 1) * 128]
                    st_, sp = (ct == 0), (ct == 7)
                    pe.matmul(ps_p2[:, 0:512], lh,
                              w_s["wvT"][:, ct * D:ct * D + 512], start=st_, stop=sp)
                    ins = pe.matmul(ps_p2[:, 512:1024], lh,
                                    w_s["wvT"][:, ct * D + 512:ct * D + 1024],
                                    start=st_, stop=sp)
                mark("pe", ins, s_pe, M_pe("vg", i))
                wt(pe, "pe", s_ac, M_ac("kaTev", i - 1))
                wt(pe, "pe", s_dv, M_dv("qb", i))
                for j in range(8):
                    ins = pe.matmul(ps_pb[:, j * 128:(j + 1) * 128],
                                    qb[p][:, j * 128:(j + 1) * 128], identb[:],
                                    start=(j == 0), stop=(j == 7), is_transpose=True)
                mark("pe", ins, s_pe, M_pe("qT", i))
                wt(pe, "pe", s_ac, C_ln1n(i + 1))
                for j in range(8):
                    ins = pe.matmul(ps_pb2[:, j * 128:(j + 1) * 128],
                                    ln1b[(i + 1) % 2][:, j * 128:(j + 1) * 128],
                                    identb[:],
                                    start=(j == 0), stop=(j == 7), is_transpose=True)
                mark("pe", ins, s_pe, M_pe("lnT", i))

                def at_quarter(q, dst):
                    for idx in range(4):
                        h = 4 * q + idx
                        lh = lnT[p][:, (h // 2) * 128:(h // 2) * 128 + 128]
                        qz = qTe if h % 2 == 0 else qTo
                        rh = qz[:, (h // 2) * 128:(h // 2) * 128 + 128]
                        ins = pe.matmul(dst[:, idx * 128:(idx + 1) * 128],
                                        lh, rh, start=(idx == 0), stop=(idx == 3))
                    return ins

                def intra_group(hs, dst):
                    for h in hs:
                        ins = pe.matmul(dst[:, (h % 8) * 64:(h % 8) * 64 + 64],
                                        amask[:, h * 128:(h + 1) * 128],
                                        vn[p][:, h * 64:h * 64 + 64],
                                        start=(h in (0, 8)), stop=False)
                    return ins

                wt(pe, "pe", s_dv, M_dv("qTev", i))
                wt(pe, "pe", s_ac, M_ac("sqQ", i))
                ins = at_quarter(0, ps_p1[:, 0:512])
                mark("pe", ins, s_pe, M_pe("at1", i))
                ins = at_quarter(1, ps_p1[:, 512:1024])
                mark("pe", ins, s_pe, M_pe("at2", i))
                wt(pe, "pe", s_dv, M_dv("vn1", i))
                wt(pe, "pe", s_ac, M_ac("sqV", i))
                ins = at_quarter(2, ps_p2[:, 0:512])
                mark("pe", ins, s_pe, M_pe("at3", i))
                ins = at_quarter(3, ps_p2[:, 512:1024])
                mark("pe", ins, s_pe, M_pe("at4", i))
                wt(pe, "pe", s_dv, M_dv("vn2", i))
                wt(pe, "pe", s_dv, M_dv("mke", i))
                intra_group(range(0, 8), ps_p1[:, 0:512])
                wt(pe, "pe", s_dv, M_dv("mko", i))
                ins = intra_group(range(8, 16), ps_p1[:, 512:1024])
                mark("pe", ins, s_pe, M_pe("intra", i))
                wt(pe, "pe", s_gp, M_gp("qtf", i))
                wt(pe, "pe", s_gp, M_gp("sbd", i - 1))
                for ct in range(8):
                    ins = pe.matmul(ps_p1[:, ct * 128:(ct + 1) * 128],
                                    qTf[:, ct * 128:(ct + 1) * 128],
                                    sbd[p][:, ct * 128:(ct + 1) * 128],
                                    start=False, stop=(ct in (3, 7)))
                mark("pe", ins, s_pe, M_pe("inter", i))
                wt(pe, "pe", s_dv, M_dv("sadd", i - 1))
                wt(pe, "pe", s_dv, M_dv("zmul", i - 1))
                for ct in range(8):
                    ins = pe.matmul(ps_s[:, ct * 128:(ct + 1) * 128],
                                    ln1b[p][:, ct * 128:(ct + 1) * 128],
                                    vn[p][:, ct * 128:(ct + 1) * 128],
                                    start=(ct in (0, 4)),
                                    stop=(ct in (3, 7)))
                mark("pe", ins, s_pe, M_pe("g", i))
                wt(pe, "pe", s_gp, M_gp("xkb", i - 1))
                wt(pe, "pe", s_dv, M_dv("mko", i))
                wt(pe, "pe", s_w, 16 * 32)
                for ct in range(8):
                    lh = xkT[:, ct * 128:(ct + 1) * 128]
                    st_, sp = (ct == 0), (ct == 7)
                    pe.matmul(ps_p2[:, 0:512], lh,
                              w_s["wkT"][:, ct * D:ct * D + 512], start=st_, stop=sp)
                    ins = pe.matmul(ps_p2[:, 512:1024], lh,
                                    w_s["wkT"][:, ct * D + 512:ct * D + 1024],
                                    start=st_, stop=sp)
                mark("pe", ins, s_pe, M_pe("kk", i))
                wt(pe, "pe", s_dv, M_dv("xr", i - 1))
                wt(pe, "pe", s_dv, M_dv("y", i))
                wt(pe, "pe", s_w, 16 * 24)
                for ct in range(8):
                    lh = xrT[:, ct * 128:(ct + 1) * 128]
                    st_, sp = (ct == 0), (ct == 7)
                    pe.matmul(ps_p1[:, 0:512], lh,
                              w_s["wrT"][:, ct * D:ct * D + 512], start=st_, stop=sp)
                    ins = pe.matmul(ps_p1[:, 512:1024], lh,
                                    w_s["wrT"][:, ct * D + 512:ct * D + 1024],
                                    start=st_, stop=sp)
                mark("pe", ins, s_pe, M_pe("rpre", i))
                wt(pe, "pe", s_ac, M_ac("kab", i))
                for j in range(8):
                    ins = pe.matmul(ps_pb[:, j * 128:(j + 1) * 128],
                                    kab[:, j * 128:(j + 1) * 128], identb[:],
                                    start=(j == 0), stop=(j == 7), is_transpose=True)
                mark("pe", ins, s_pe, M_pe("kaT", i))
                wt(pe, "pe", s_dv, M_dv("nrm2", i))
                wt(pe, "pe", s_ac, M_ac("ybs", i))
                wt(pe, "pe", s_ac, M_ac("sig", i))
                for j in range(8):
                    ins = pe.matmul(ps_p1[:, j * 128:(j + 1) * 128],
                                    ybs[:, j * 128:(j + 1) * 128], diagr[:],
                                    start=(j in (0, 4)), stop=(j in (3, 7)))
                mark("pe", ins, s_pe, M_pe("ln2T", i))
                wt(pe, "pe", s_dv, M_dv("sadd", i))
                wt(pe, "pe", s_w, 16 * 40)
                for ct in range(8):
                    lh = kaT[:, ct * 128:(ct + 1) * 128]
                    st_, sp = (ct == 0), (ct == 7)
                    pe.matmul(ps_s[:, 0:512], lh,
                              w_s["woT"][:, ct * D:ct * D + 512], start=st_, stop=sp)
                    ins = pe.matmul(ps_s[:, 512:1024], lh,
                                    w_s["woT"][:, ct * D + 512:ct * D + 1024],
                                    start=st_, stop=sp)
                mark("pe", ins, s_pe, M_pe("out3", i))
            tot_dv = M_dv("zmul", N - 1)
            tot_ac = M_ac("bcopy", N - 1)
            wt(pe, "pe", s_gp, M_gp("xkb", N - 1))
            wt(pe, "pe", s_ac, M_ac("relu", N - 1))
            for ct in range(8):
                lh = xkT[:, ct * 128:(ct + 1) * 128]
                st_, sp = (ct == 0), (ct == 7)
                pe.matmul(ps_p2[:, 0:512], lh,
                          w_s["wkT"][:, ct * D:ct * D + 512], start=st_, stop=sp)
                ins = pe.matmul(ps_p2[:, 512:1024], lh,
                                w_s["wkT"][:, ct * D + 512:ct * D + 1024],
                                start=st_, stop=sp)
            ins.then_inc(s_pe, 1)
            wt(pe, "pe", s_dv, M_dv("xr", N - 1))
            wt(pe, "pe", s_ac, M_ac("ln2Tev", N - 1))
            for ct in range(8):
                lh = xrT[:, ct * 128:(ct + 1) * 128]
                st_, sp = (ct == 0), (ct == 7)
                pe.matmul(ps_p1[:, 0:512], lh,
                          w_s["wrT"][:, ct * D:ct * D + 512], start=st_, stop=sp)
                ins = pe.matmul(ps_p1[:, 512:1024], lh,
                                w_s["wrT"][:, ct * D + 512:ct * D + 1024],
                                start=st_, stop=sp)
            ins.then_inc(s_pe, 1)
            pe.wait_ge(s_ac, tot_ac + 2)
            for j in range(8):
                ins = pe.matmul(ps_pb[:, j * 128:(j + 1) * 128],
                                kab[:, j * 128:(j + 1) * 128], identb[:],
                                start=(j == 0), stop=(j == 7), is_transpose=True)
            ins.then_inc(s_pe, 1)
            pe.wait_ge(s_ac, tot_ac + 4)
            wt(pe, "pe", s_dv, tot_dv)
            for ct in range(8):
                lh = kaT[:, ct * 128:(ct + 1) * 128]
                st_, sp = (ct == 0), (ct == 7)
                pe.matmul(ps_s[:, 0:512], lh,
                          w_s["woT"][:, ct * D:ct * D + 512], start=st_, stop=sp)
                ins = pe.matmul(ps_s[:, 512:1024], lh,
                                w_s["woT"][:, ct * D + 512:ct * D + 1024],
                                start=st_, stop=sp)
            ins.then_inc(s_pe, 1)

    return nc


def make_host_inputs(Wq, Wv, qw, vw, Wr, Wk, Wo, mur, muk):
    f32 = np.float32
    shared = {
        "wqT": np.ascontiguousarray(np.asarray(Wq, dtype=f32).T),
        "wvT": np.ascontiguousarray(np.asarray(Wv, dtype=f32).T),
        "wrT": np.ascontiguousarray(np.asarray(Wr, dtype=f32).T),
        "wkT": np.ascontiguousarray(np.asarray(Wk, dtype=f32).T),
        "woT": np.ascontiguousarray(np.asarray(Wo, dtype=f32).T),
        "identb": np.eye(128, dtype=bfloat16),
        "qwrep": np.ascontiguousarray(
            np.broadcast_to(np.asarray(qw, dtype=f32).reshape(1, D), (128, D))),
        "vwrep": np.ascontiguousarray(
            np.broadcast_to(np.asarray(vw, dtype=f32).reshape(1, D), (128, D))),
        "maskrep": np.ascontiguousarray(
            np.tile(np.triu(np.ones((128, 128), dtype=f32)), (1, 8))
        ).astype(bfloat16),
        "murF": np.ascontiguousarray(np.asarray(mur, dtype=f32).reshape(8, 128).T),
        "mukF": np.ascontiguousarray(np.asarray(muk, dtype=f32).reshape(8, 128).T),
        "consts": np.ascontiguousarray(np.broadcast_to(
            np.array([1e-5, 1e-8, 1.0 / 64, 1.0], dtype=f32), (128, 4))),
    }
    return shared


_NC_CACHE = {}


def kernel(L0_attn_X, L0_attn_Wq, L0_attn_Wv, L0_attn_q_norm_w, L0_attn_v_norm_w,
           L0_ffn_W_r, L0_ffn_W_k, L0_ffn_W_v, L0_ffn_mu_r, L0_ffn_mu_k):
    X = np.ascontiguousarray(np.asarray(L0_attn_X, dtype=np.float32))
    B, L, _ = X.shape
    n_chunks = L // C
    if n_chunks not in _NC_CACHE:
        _NC_CACHE[n_chunks] = build_program(n_chunks)
    nc = _NC_CACHE[n_chunks]
    shared = make_host_inputs(L0_attn_Wq, L0_attn_Wv, L0_attn_q_norm_w,
                              L0_attn_v_norm_w, L0_ffn_W_r, L0_ffn_W_k,
                              L0_ffn_W_v, L0_ffn_mu_r, L0_ffn_mu_k)
    in_maps = []
    for b in range(B):
        m = {"x": np.ascontiguousarray(X[b])}
        m.update(shared)
        in_maps.append(m)
    res = run_bass_kernel_spmd(nc, in_maps, core_ids=list(range(B)))
    out = np.stack([np.asarray(res.results[b]["z"], dtype=np.float32)
                    for b in range(B)], axis=0)
    return out


# version 30
# speedup vs baseline: 1.2316x; 1.2316x over previous
import numpy as np
import ml_dtypes
from contextlib import ExitStack

import concourse.bass as bass
import concourse.mybir as mybir
from concourse.bass_utils import run_bass_kernel_spmd

F32 = mybir.dt.float32
BF16 = mybir.dt.bfloat16
AF = mybir.ActivationFunctionType
ALU = mybir.AluOpType
AX = mybir.AxisListType
bfloat16 = ml_dtypes.bfloat16

D = 1024
H = 16
HD = 64
C = 128
N_CORES = 8

PE_LABELS = ["qg", "vg", "qT", "at1", "at2", "at3", "at4",
             "intra", "inter", "g", "kk", "rpre", "kaT", "lnT", "ln2T", "out3"]
DV_LABELS = ["qb", "msq", "qTev", "msv", "vn1", "mke", "vn2",
             "mko", "var1", "nrm1", "sadd", "y", "var2", "nrm2", "mixd", "xr", "zmul"]
AC_LABELS = ["sumsq", "sqQ", "sqV", "sqrtQ", "sqrtV", "sqrt1", "ln1n",
             "relu", "kab", "ybs", "ysq", "sqrt2", "kaTev", "sig",
             "ln2Tev", "lnTev", "bcopy"]
GP_LABELS = ["qtf", "sbd", "xkb", "z"]

NPE, NDV, NAC, NGP = map(len, (PE_LABELS, DV_LABELS, AC_LABELS, GP_LABELS))
PE_PRO = 1
DV_PRO = 3
AC_PRO = 4


def M_pe(lbl, i): return PE_PRO + i * NPE + PE_LABELS.index(lbl) + 1
def M_dv(lbl, i): return DV_PRO + i * NDV + DV_LABELS.index(lbl) + 1
def M_ac(lbl, i): return AC_PRO + i * NAC + AC_LABELS.index(lbl) + 1
def M_gp(lbl, i): return i * NGP + GP_LABELS.index(lbl) + 1


def C_ln1n(i):
    return 3 if i == 0 else M_ac("ln1n", i - 1)


def C_lnTev(i):
    return AC_PRO if i == 0 else M_ac("lnTev", i - 1)


def build_program(n_chunks, dbg=False):
    N = n_chunks
    ROWS = N * C
    nc = bass.Bass("TRN2", target_bir_lowering=False, debug=False,
                   detect_race_conditions=False)

    xd = nc.dram_tensor("x", [ROWS, D], F32, kind="ExternalInput").ap()
    zd = nc.dram_tensor("z", [ROWS, D], F32, kind="ExternalOutput").ap()
    w_d = {n: nc.dram_tensor(n, [D, D], F32, kind="ExternalInput").ap()
           for n in ("wqT", "wvT", "wrT", "wkT", "woT")}
    identb_d = nc.dram_tensor("identb", [128, 128], BF16, kind="ExternalInput").ap()
    qwrep_d = nc.dram_tensor("qwrep", [128, D], F32, kind="ExternalInput").ap()
    vwrep_d = nc.dram_tensor("vwrep", [128, D], F32, kind="ExternalInput").ap()
    mask_d = nc.dram_tensor("maskrep", [128, D], BF16, kind="ExternalInput").ap()
    murF_d = nc.dram_tensor("murF", [128, 8], F32, kind="ExternalInput").ap()
    mukF_d = nc.dram_tensor("mukF", [128, 8], F32, kind="ExternalInput").ap()
    consts_d = nc.dram_tensor("consts", [128, 4], F32, kind="ExternalInput").ap()
    N_CONST_DMAS = 7
    dbg_d = {}

    with ExitStack() as ST:
        def sb(name, shape, dtype=F32):
            return ST.enter_context(nc.sbuf_tensor(name, shape, dtype))

        def psum(name, shape, dtype=F32):
            return ST.enter_context(nc.psum_tensor(name, shape, dtype))

        def sem(name):
            return ST.enter_context(nc.semaphore(name))

        xs = [sb("xs0", [128, D]), sb("xs1", [128, D])]
        zs = sb("zs", [128, D])
        ys = [sb("ys0", [128, D]), sb("ys1", [128, D])]
        ln1b = [sb("ln1b0", [128, D], BF16), sb("ln1b1", [128, D], BF16)]
        lnT = [sb("lnT0", [128, D], BF16), sb("lnT1", [128, D], BF16)]
        qb = [sb("qb0", [128, D], BF16), sb("qb1", [128, D], BF16)]
        qTe = sb("qTe", [128, D], BF16)
        qTo = sb("qTo", [128, D], BF16)
        qTf = sb("qTf", [128, D], BF16)
        vn = [sb("vn0", [128, D], BF16), sb("vn1_", [128, D], BF16)]
        amask = sb("am", [128, 2048], BF16)
        ybs = sb("ybs", [128, D], BF16)
        diagr = sb("diagr", [128, 128], BF16)
        ln2T = [sb("ln2T0", [128, 8 * 129], BF16), sb("ln2T1", [128, 8 * 129], BF16)]
        kaT = sb("kaT", [128, D], BF16)
        sbd = [sb("sbd0", [128, D], BF16), sb("sbd1", [128, D], BF16)]
        Ssum = sb("Ssum", [128, D])
        rs = sb("rs", [128, D])
        kab = sb("kab", [128, D], BF16)
        sqB = sb("sqB", [128, D])
        sqBv = sb("sqBv", [128, D])
        tscr = sb("tscr", [128, D])
        vscr = sb("vscr", [128, D])
        y1scr = sb("y1scr", [128, D])
        w_s = {n: sb(n + "_s", [128, 8 * D], BF16) for n in w_d}
        xrT = sb("xrT", [128, D], BF16)
        xkT = sb("xkT", [128, D], BF16)
        dscr = sb("dscr", [128, D], BF16)
        t1scr = sb("t1scr", [128, D], BF16)
        t1g = sb("t1g", [128, D], BF16)
        murFb = sb("murFb", [128, 8], BF16)
        mukFb = sb("mukFb", [128, 8], BF16)
        identb = sb("identb_s", [128, 128], BF16)
        qwrep = sb("qwrep_s", [128, D])
        vwrep = sb("vwrep_s", [128, D])
        maskrep = sb("maskrep_s", [128, D], BF16)
        murF = sb("murF_s", [128, 8])
        mukF = sb("mukF_s", [128, 8])
        consts = sb("consts_s", [128, 4])
        tin = {n: sb(n, [128, 2]) for n in
               ("s1", "mean1", "m2", "var1", "sqv1", "rstd1", "b1")}
        tin["ss1"] = sb("ss1", [128, 4])
        for a, b in (("s2", "s1"), ("nmean2", "mean1"), ("m22", "m2"),
                     ("var2", "var1"), ("sqv2", "sqv1"), ("rstd2", "rstd1"),
                     ("ss2", "ss1")):
            tin[a] = tin[b]
        hh = {n: sb(n, [128, 32]) for n in
              ("msq", "msv", "sqmsQ", "sqmsV", "rsfq", "rsv")}

        ps_s = psum("ps_s", [128, D])
        ps_p1 = psum("ps_p1", [128, D])
        ps_p2 = psum("ps_p2", [128, D])
        ps_pb = psum("ps_pb", [128, D], BF16)
        ps_pb2 = psum("ps_pb2", [128, D], BF16)

        s_x = [sem("s_x0"), sem("s_x1")]
        s_w = sem("s_w"); s_cst = sem("s_cst")
        s_z = sem("s_z")
        s_pe = sem("s_pe"); s_dv = sem("s_dv"); s_ac = sem("s_ac"); s_gp = sem("s_gp")
        s_fd = sem("s_fd")
        fence_ct = [0]

        def fence(dv_eng, ins):
            fence_ct[0] += 1
            ins.then_inc(s_fd, 1)
            dv_eng.wait_ge(s_fd, fence_ct[0])

        hw = {}

        def wt(eng, key, s, v):
            if v <= 0:
                return
            k = (key, id(s))
            if hw.get(k, 0) >= v:
                return
            hw[k] = v
            eng.wait_ge(s, v)

        emitted = {"pe": 0, "dv": 0, "ac": 0, "gp": 0}

        def mark(key, instr, sem_h, val):
            assert val == emitted[key] + 1, (key, val, emitted[key])
            emitted[key] = val
            instr.then_inc(sem_h, 1)

        def xcount(c):
            return 16 * (c // 2 + 1)

        block = ST.enter_context(nc.Block())

        @block.sync
        def _(sync):
            for ap_s, ap_d in ((identb, identb_d),
                               (qwrep, qwrep_d), (vwrep, vwrep_d),
                               (maskrep, mask_d), (murF, murF_d),
                               (mukF, mukF_d), (consts, consts_d)):
                sync.dma_start(out=ap_s[:], in_=ap_d).then_inc(s_cst, 16)
            sync.dma_start(out=xs[0][:], in_=xd[0:C, :]).then_inc(s_x[0], 16)
            if N > 1:
                sync.dma_start(out=xs[1][:], in_=xd[C:2 * C, :]).then_inc(s_x[1], 16)
            for i in range(N):
                if i + 2 < N:
                    wt(sync, "sy", s_dv, M_dv("y", i))
                    wt(sync, "sy", s_ac, C_ln1n(i))
                    sync.dma_start(out=xs[i % 2][:],
                                   in_=xd[(i + 2) * C:(i + 3) * C, :]).then_inc(s_x[i % 2], 16)
                if i >= 1:
                    wt(sync, "sy", s_gp, M_gp("z", i))
                    sync.dma_start(out=zd[(i - 1) * C:i * C, :],
                                   in_=zs[:]).then_inc(s_z, 16)
            wt(sync, "sy", s_gp, M_gp("z", N - 1) + 1)
            sync.dma_start(out=zd[(N - 1) * C:N * C, :],
                           in_=zs[:]).then_inc(s_z, 16)
            sync.wait_ge(s_z, 16 * N)
            if dbg:
                q_ = (N - 1) % 2
                dbg_map = {"ln1b": ln1b[q_], "lnT": lnT[q_], "qb": qb[q_],
                           "qTe": qTe, "qTo": qTo, "qTf": qTf, "vn": vn[q_],
                           "amask": amask,
                           "ys0": ys[0][:], "ys1": ys[1][:], "Ssum": Ssum,
                           "sbd0": sbd[0][:], "sbd1": sbd[1][:], "rs": rs,
                           "kab": kab, "kaT": kaT, "ybs": ybs,
                           "ln2T0": ln2T[0][:], "ln2T1": ln2T[1][:],
                           "tscr": tscr, "y1scr": y1scr, "sqB": sqB}
                sync.wait_ge(s_dv, M_dv("zmul", N - 1) + 1)
                sync.wait_ge(s_ac, M_ac("bcopy", N - 1) + 4)
                sync.wait_ge(s_gp, M_gp("z", N - 1) + 1)
                nd = 0
                for nm, t in dbg_map.items():
                    ap = t[:] if hasattr(t, "shape") and not isinstance(t, bass.AP) else t
                    dd = nc.dram_tensor("dbg_" + nm, list(ap.shape), ap.dtype,
                                        kind="ExternalOutput").ap()
                    dbg_d[nm] = dd
                    sync.dma_start(out=dd, in_=ap).then_inc(s_cst, 16)
                    nd += 1
                sync.wait_ge(s_cst, 16 * (N_CONST_DMAS + nd))

        @block.gpsimd
        def _(gp):
            for n in ("wqT", "wvT", "wrT", "wkT", "woT"):
                for ct in range(8):
                    gp.dma_start(out=w_s[n][:, ct * D:(ct + 1) * D],
                                 in_=w_d[n][ct * 128:(ct + 1) * 128, :]
                                 ).then_inc(s_w, 16)
            for i in range(N):
                wt(gp, "gp", s_cst, 16 * N_CONST_DMAS)
                wt(gp, "gp", s_dv, 1)
                wt(gp, "gp", s_dv, M_dv("qTev", i))
                wt(gp, "gp", s_pe, M_pe("inter", i - 1))
                ins = gp.tensor_add(qTf[:], qTe[:], qTo[:])
                mark("gp", ins, s_gp, M_gp("qtf", i))
                wt(gp, "gp", s_dv, M_dv("sadd", i))
                sB = Ssum[0:64, :].rearrange("p (ct c) -> p ct c", ct=8)
                dB = sbd[(i + 1) % 2][0:64, :].rearrange("p (ct c) -> p ct c", ct=8)
                gp.tensor_copy(out=dB[:, :, 0:64], in_=sB[:, :, 0:64])
                sB = Ssum[64:128, :].rearrange("p (ct c) -> p ct c", ct=8)
                dB = sbd[(i + 1) % 2][64:128, :].rearrange("p (ct c) -> p ct c", ct=8)
                ins = gp.tensor_copy(out=dB[:, :, 64:128], in_=sB[:, :, 64:128])
                mark("gp", ins, s_gp, M_gp("sbd", i))
                wt(gp, "gp", s_dv, M_dv("mixd", i))
                wt(gp, "gp", s_pe, M_pe("kk", i))
                lt2 = ln2T[i % 2][:].rearrange("p (a b) -> p a b", a=8)
                xpv = lt2[:, :, 0:128]
                d3 = dscr[:].rearrange("p (a b) -> p a b", a=8)
                t13g = t1g[:].rearrange("p (a b) -> p a b", a=8)
                mukb = mukFb[:].unsqueeze(2).broadcast_to((128, 8, C))
                gp.tensor_tensor(out=t13g, in0=d3, in1=mukb, op=ALU.mult)
                ins = gp.tensor_tensor(out=xkT[:].rearrange("p (a b) -> p a b", a=8),
                                       in0=xpv, in1=t13g, op=ALU.add)
                mark("gp", ins, s_gp, M_gp("xkb", i))
                wt(gp, "gp", s_dv, M_dv("zmul", i))
                wt(gp, "gpz", s_z, 16 * (i - 1))
                ins = gp.tensor_tensor(out=zs[:], in0=y1scr[:],
                                       in1=ys[(i - 1) % 2][:], op=ALU.add)
                mark("gp", ins, s_gp, M_gp("z", i))
            gp.wait_ge(s_dv, M_dv("zmul", N - 1) + 1)
            wt(gp, "gpz", s_z, 16 * (N - 1))
            ins = gp.tensor_tensor(out=zs[:], in0=y1scr[:],
                                   in1=ys[(N - 1) % 2][:], op=ALU.add)
            ins.then_inc(s_gp, 1)

        @block.scalar
        def _(ac):
            def ln1_next(c):
                pc = c % 2
                if c < N:
                    wt(ac, "ac%d" % pc, s_x[pc], xcount(c))
                src = xs[pc]
                ac.activation(out=tscr[:, 0:512], in_=src[:, 0:512],
                              func=AF.Square,
                              accum_out=tin["ss1"][:, 2 * pc:2 * pc + 1])
                ins = ac.activation(out=tscr[:, 512:1024], in_=src[:, 512:1024],
                                    func=AF.Square,
                                    accum_out=tin["ss1"][:, 2 * pc + 1:2 * pc + 2])
                yield ins
                ins = ac.activation(out=tin["sqv1"][:, pc:pc + 1],
                                    in_=tin["var1"][:, pc:pc + 1], func=AF.Sqrt,
                                    bias=consts[:, 0:1])
                yield ins
                ins = ac.activation(out=ln1b[pc][:], in_=src[:], func=AF.Identity,
                                    bias=tin["b1"][:, pc:pc + 1],
                                    scale=tin["rstd1"][:, pc:pc + 1])
                yield ins

            wt(ac, "ac", s_cst, 16 * N_CONST_DMAS)
            g0 = ln1_next(0)
            ins = next(g0)
            ins.then_inc(s_ac, 1)
            wt(ac, "ac", s_dv, 2)
            ins = next(g0)
            ins.then_inc(s_ac, 2 - 1)
            wt(ac, "ac", s_dv, 3)
            ins = next(g0)
            ins.then_inc(s_ac, 1)
            wt(ac, "ac", s_pe, PE_PRO)
            ins = ac.copy(out=lnT[0][:], in_=ps_pb2[:])
            ins.then_inc(s_ac, 1)
            emitted["ac"] = AC_PRO

            for i in range(N):
                p = i % 2
                g1 = ln1_next(i + 1)
                ins = next(g1)
                mark("ac", ins, s_ac, M_ac("sumsq", i))
                wt(ac, "ac", s_pe, M_pe("qg", i))
                ins = ac.activation(out=sqB[:], in_=ps_p1[:], func=AF.Square)
                mark("ac", ins, s_ac, M_ac("sqQ", i))
                wt(ac, "ac", s_pe, M_pe("vg", i))
                ins = ac.activation(out=sqBv[:], in_=ps_p2[:], func=AF.Square)
                mark("ac", ins, s_ac, M_ac("sqV", i))
                wt(ac, "ac", s_dv, M_dv("msq", i))
                ins = ac.activation(out=hh["sqmsQ"][:, p * 16:(p + 1) * 16],
                                    in_=hh["msq"][:, p * 16:(p + 1) * 16],
                                    func=AF.Sqrt, bias=consts[:, 1:2],
                                    scale=consts[:, 2:3])
                mark("ac", ins, s_ac, M_ac("sqrtQ", i))
                wt(ac, "ac", s_dv, M_dv("msv", i))
                ins = ac.activation(out=hh["sqmsV"][:, p * 16:(p + 1) * 16],
                                    in_=hh["msv"][:, p * 16:(p + 1) * 16],
                                    func=AF.Sqrt, bias=consts[:, 1:2],
                                    scale=consts[:, 2:3])
                mark("ac", ins, s_ac, M_ac("sqrtV", i))
                wt(ac, "ac", s_dv, M_dv("var1", i))
                ins = next(g1)
                mark("ac", ins, s_ac, M_ac("sqrt1", i))
                wt(ac, "ac", s_dv, M_dv("nrm1", i))
                wt(ac, "ac", s_pe, M_pe("g", i - 1))
                ins = next(g1)
                mark("ac", ins, s_ac, M_ac("ln1n", i))
                wt(ac, "ac", s_pe, M_pe("kk", i))
                ins = ac.activation(out=tscr[:], in_=ps_p2[:], func=AF.Relu)
                mark("ac", ins, s_ac, M_ac("relu", i))
                wt(ac, "ac", s_pe, M_pe("kaT", i - 1))
                ins = ac.activation(out=kab[:], in_=tscr[:], func=AF.Square)
                mark("ac", ins, s_ac, M_ac("kab", i))
                wt(ac, "ac", s_dv, M_dv("y", i))
                wt(ac, "ac", s_pe, M_pe("ln2T", i - 1))
                ins = ac.activation(out=ybs[:], in_=ys[p][:], func=AF.Identity,
                                    bias=tin["nmean2"][:, p:p + 1])
                mark("ac", ins, s_ac, M_ac("ybs", i))
                ac.activation(out=tscr[:, 0:512], in_=ys[p][:, 0:512],
                              func=AF.Square,
                              accum_out=tin["ss2"][:, 2 * p:2 * p + 1])
                ins = ac.activation(out=tscr[:, 512:1024], in_=ys[p][:, 512:1024],
                                    func=AF.Square,
                                    accum_out=tin["ss2"][:, 2 * p + 1:2 * p + 2])
                mark("ac", ins, s_ac, M_ac("ysq", i))
                wt(ac, "ac", s_dv, M_dv("var2", i))
                ins = ac.activation(out=tin["sqv2"][:, p:p + 1],
                                    in_=tin["var2"][:, p:p + 1], func=AF.Sqrt,
                                    bias=consts[:, 0:1])
                mark("ac", ins, s_ac, M_ac("sqrt2", i))
                wt(ac, "ac", s_pe, M_pe("kaT", i))
                wt(ac, "ac", s_pe, M_pe("out3", i - 1))
                ins = ac.copy(out=kaT[:], in_=ps_pb[:])
                mark("ac", ins, s_ac, M_ac("kaTev", i))
                wt(ac, "ac", s_pe, M_pe("rpre", i))
                wt(ac, "ac", s_dv, M_dv("zmul", i - 1))
                ins = ac.activation(out=rs[:], in_=ps_p1[:], func=AF.Sigmoid)
                mark("ac", ins, s_ac, M_ac("sig", i))
                wt(ac, "ac", s_pe, M_pe("ln2T", i))
                wt(ac, "ac", s_dv, M_dv("xr", i - 2))
                wt(ac, "ac", s_gp, M_gp("xkb", i - 2))
                dst = ln2T[p][:].rearrange("p (a b) -> p a b", a=8)[:, :, 1:129]
                ins = ac.copy(out=dst, in_=ps_p2[:].rearrange("p (a b) -> p a b", a=8))
                mark("ac", ins, s_ac, M_ac("ln2Tev", i))
                wt(ac, "ac", s_pe, M_pe("lnT", i))
                wt(ac, "ac", s_pe, M_pe("at4", i - 1))
                ins = ac.copy(out=lnT[(i + 1) % 2][:], in_=ps_pb2[:])
                mark("ac", ins, s_ac, M_ac("lnTev", i))
                wt(ac, "ac", s_dv, M_dv("xr", i - 1))
                wt(ac, "ac", s_gp, M_gp("xkb", i - 1))
                src = ln2T[p][:].rearrange("p (a b) -> p a b", a=8)[:, :, 128:129]
                dst = ln2T[(i + 1) % 2][:].rearrange("p (a b) -> p a b", a=8)[:, :, 0:1]
                ins = ac.copy(out=dst, in_=src)
                mark("ac", ins, s_ac, M_ac("bcopy", i))
            tot_pe = M_pe("out3", N - 1)
            tot_dv = M_dv("zmul", N - 1)
            ac.wait_ge(s_pe, tot_pe + 1)
            ac.activation(out=tscr[:], in_=ps_p2[:],
                          func=AF.Relu).then_inc(s_ac, 1)
            ac.activation(out=kab[:], in_=tscr[:],
                          func=AF.Square).then_inc(s_ac, 1)
            ac.wait_ge(s_pe, tot_pe + 2)
            ac.wait_ge(s_dv, tot_dv)
            ac.activation(out=rs[:], in_=ps_p1[:],
                          func=AF.Sigmoid).then_inc(s_ac, 1)
            ac.wait_ge(s_pe, tot_pe + 3)
            ac.copy(out=kaT[:], in_=ps_pb[:]).then_inc(s_ac, 1)

        @block.vector
        def _(dv):

            def ln1_stats(c):
                pc = c % 2
                if c < N:
                    wt(dv, "dv%d" % pc, s_x[pc], xcount(c))
                ins = dv.tensor_reduce(out=tin["s1"][:, pc:pc + 1], in_=xs[pc][:],
                                       axis=AX.X, op=ALU.add)
                fence(dv, ins)
                ins = dv.tensor_single_scalar(out=tin["mean1"][:, pc:pc + 1],
                                              in_=tin["s1"][:, pc:pc + 1],
                                              scalar=1.0 / D, op=ALU.mult)
                fence(dv, ins)
                dv.tensor_tensor(out=tin["m2"][:, pc:pc + 1],
                                 in0=tin["mean1"][:, pc:pc + 1],
                                 in1=tin["mean1"][:, pc:pc + 1], op=ALU.mult)
                ins = dv.tensor_add(tin["ss1"][:, 2 * pc:2 * pc + 1],
                                    tin["ss1"][:, 2 * pc:2 * pc + 1],
                                    tin["ss1"][:, 2 * pc + 1:2 * pc + 2])
                fence(dv, ins)
                ins = dv.scalar_tensor_tensor(out=tin["var1"][:, pc:pc + 1],
                                              in0=tin["ss1"][:, 2 * pc:2 * pc + 1],
                                              scalar=1.0 / D,
                                              in1=tin["m2"][:, pc:pc + 1],
                                              op0=ALU.mult, op1=ALU.subtract)
                yield ins
                ins = dv.reciprocal(out=tin["rstd1"][:, pc:pc + 1],
                                    in_=tin["sqv1"][:, pc:pc + 1])
                fence(dv, ins)
                ins = dv.scalar_tensor_tensor(out=tin["b1"][:, pc:pc + 1],
                                              in0=tin["mean1"][:, pc:pc + 1],
                                              scalar=-1.0,
                                              in1=tin["rstd1"][:, pc:pc + 1],
                                              op0=ALU.mult, op1=ALU.mult)
                yield ins

            dv.memset(sbd[0][:], 0.0)
            dv.memset(sbd[1][:], 0.0)
            dv.memset(Ssum[:], 0.0)
            dv.memset(qTe[:], 0.0)
            dv.memset(qTo[:], 0.0)
            dv.memset(ys[1][:], 0.0)
            dv.memset(xrT[:], 0.0)
            dv.memset(xkT[:], 0.0)
            wt(dv, "dv", s_cst, 16 * N_CONST_DMAS)
            ins = dv.memset(
                ln2T[0][:].rearrange("p (a b) -> p a b", a=8)[:, :, 0:1], 0.0)
            ins.then_inc(s_dv, 1)
            g0 = ln1_stats(0)
            wt(dv, "dv", s_ac, 1)
            ins = next(g0)
            ins.then_inc(s_dv, 1)
            wt(dv, "dv", s_ac, 2)
            ins = next(g0)
            ins.then_inc(s_dv, 1)
            dv.tensor_copy(out=murFb[:], in_=murF[:])
            dv.tensor_copy(out=mukFb[:], in_=mukF[:])
            emitted["dv"] = DV_PRO

            for i in range(N):
                p = i % 2
                g1 = ln1_stats(i + 1)
                wt(dv, "dv", s_ac, M_ac("sqQ", i))
                wt(dv, "dv", s_pe, M_pe("qT", i - 2))
                ins = dv.tensor_tensor(out=qb[p][:], in0=ps_p1[:], in1=qwrep[:],
                                       op=ALU.mult)
                mark("dv", ins, s_dv, M_dv("qb", i))
                ins = dv.tensor_reduce(out=hh["msq"][:, p * 16:(p + 1) * 16],
                                       in_=sqB[:].rearrange("p (h e) -> p h e", h=16),
                                       axis=AX.X, op=ALU.add)
                mark("dv", ins, s_dv, M_dv("msq", i))
                wt(dv, "dv", s_pe, M_pe("qT", i))
                wt(dv, "dv", s_pe, M_pe("at4", i - 1))
                wt(dv, "dv", s_gp, M_gp("qtf", i - 1))
                dv.tensor_copy(out=qTe[0:64, :], in_=ps_pb[0:64, :])
                ins = dv.tensor_copy(out=qTo[64:128, :], in_=ps_pb[64:128, :])
                mark("dv", ins, s_dv, M_dv("qTev", i))
                wt(dv, "dv", s_ac, M_ac("sqV", i))
                ins = dv.tensor_reduce(out=hh["msv"][:, p * 16:(p + 1) * 16],
                                       in_=sqBv[:].rearrange("p (h e) -> p h e", h=16),
                                       axis=AX.X, op=ALU.add)
                mark("dv", ins, s_dv, M_dv("msv", i))
                wt(dv, "dv", s_pe, M_pe("vg", i))
                ins = dv.tensor_tensor(out=vscr[:], in0=ps_p2[:], in1=vwrep[:],
                                       op=ALU.mult)
                mark("dv", ins, s_dv, M_dv("vn1", i))
                wt(dv, "dv", s_pe, M_pe("intra", i - 1))
                wt(dv, "dv", s_pe, M_pe("at2", i))
                ins = dv.tensor_tensor(out=amask[:, 0:1024], in0=ps_p1[:],
                                       in1=maskrep[:], op=ALU.mult)
                mark("dv", ins, s_dv, M_dv("mke", i))
                wt(dv, "dv", s_ac, M_ac("sqrtV", i))
                ins = dv.reciprocal(out=hh["rsv"][:, p * 16:(p + 1) * 16],
                                    in_=hh["sqmsV"][:, p * 16:(p + 1) * 16])
                fence(dv, ins)
                wt(dv, "dv", s_pe, M_pe("g", i - 2))
                rsvb = hh["rsv"][:, p * 16:(p + 1) * 16].unsqueeze(2) \
                    .broadcast_to((128, 16, 64))
                ins = dv.tensor_tensor(out=vn[p][:].rearrange("p (h e) -> p h e", h=16),
                                       in0=vscr[:].rearrange("p (h e) -> p h e", h=16),
                                       in1=rsvb, op=ALU.mult)
                mark("dv", ins, s_dv, M_dv("vn2", i))
                wt(dv, "dv", s_pe, M_pe("at4", i))
                ins = dv.tensor_tensor(out=amask[:, 1024:2048], in0=ps_p2[:],
                                       in1=maskrep[:], op=ALU.mult)
                mark("dv", ins, s_dv, M_dv("mko", i))
                wt(dv, "dv", s_ac, M_ac("sumsq", i))
                ins = next(g1)
                mark("dv", ins, s_dv, M_dv("var1", i))
                wt(dv, "dv", s_ac, M_ac("sqrt1", i))
                ins = next(g1)
                mark("dv", ins, s_dv, M_dv("nrm1", i))
                wt(dv, "dv", s_pe, M_pe("g", i))
                wt(dv, "dv", s_gp, M_gp("sbd", i - 1))
                ins = dv.tensor_tensor(out=Ssum[:], in0=Ssum[:], in1=ps_s[:],
                                       op=ALU.add)
                mark("dv", ins, s_dv, M_dv("sadd", i))
                wt(dv, "dv", s_ac, M_ac("sqrtQ", i))
                ins = dv.reciprocal(out=hh["rsfq"][:, p * 16:(p + 1) * 16],
                                    in_=hh["sqmsQ"][:, p * 16:(p + 1) * 16])
                fence(dv, ins)
                wt(dv, "dv", s_pe, M_pe("inter", i))
                wt(dv, "dv", s_gp, M_gp("z", i - 1))
                rsfb = hh["rsfq"][:, p * 16:(p + 1) * 16].unsqueeze(2) \
                    .broadcast_to((128, 16, 64))
                dv.tensor_tensor(out=y1scr[:].rearrange("p (h e) -> p h e", h=16),
                                 in0=ps_p1[:].rearrange("p (h e) -> p h e", h=16),
                                 in1=rsfb, op=ALU.mult)
                dv.scalar_tensor_tensor(out=ys[p][:], in0=y1scr[:],
                                        scalar=1.0, in1=xs[p][:],
                                        op0=ALU.mult, op1=ALU.add,
                                        accum_out=tin["s2"][:, p:p + 1])
                ins = dv.tensor_single_scalar(out=tin["nmean2"][:, p:p + 1],
                                              in_=tin["s2"][:, p:p + 1],
                                              scalar=-1.0 / D, op=ALU.mult)
                mark("dv", ins, s_dv, M_dv("y", i))
                ins = dv.scalar_tensor_tensor(out=tin["m22"][:, p:p + 1],
                                              in0=tin["s2"][:, p:p + 1],
                                              scalar=1.0 / (D * D),
                                              in1=tin["s2"][:, p:p + 1],
                                              op0=ALU.mult, op1=ALU.mult)
                fence(dv, ins)
                wt(dv, "dv", s_ac, M_ac("ysq", i))
                ins = dv.tensor_add(tin["ss2"][:, 2 * p:2 * p + 1],
                                    tin["ss2"][:, 2 * p:2 * p + 1],
                                    tin["ss2"][:, 2 * p + 1:2 * p + 2])
                fence(dv, ins)
                ins = dv.scalar_tensor_tensor(out=tin["var2"][:, p:p + 1],
                                              in0=tin["ss2"][:, 2 * p:2 * p + 1],
                                              scalar=1.0 / D,
                                              in1=tin["m22"][:, p:p + 1],
                                              op0=ALU.mult, op1=ALU.subtract)
                mark("dv", ins, s_dv, M_dv("var2", i))
                wt(dv, "dv", s_ac, M_ac("sqrt2", i))
                ins = dv.reciprocal(out=tin["rstd2"][:, p:p + 1],
                                    in_=tin["sqv2"][:, p:p + 1])
                fence(dv, ins)
                ins = dv.tensor_single_scalar(out=diagr[:], in_=identb[:],
                                              scalar=tin["rstd2"][:, p:p + 1],
                                              op=ALU.mult)
                mark("dv", ins, s_dv, M_dv("nrm2", i))
                wt(dv, "dv", s_ac, M_ac("ln2Tev", i))
                wt(dv, "dv", s_gp, M_gp("xkb", i - 1))
                lt2 = ln2T[p][:].rearrange("p (a b) -> p a b", a=8)
                xv3 = lt2[:, :, 1:129]
                xpv = lt2[:, :, 0:128]
                d3 = dscr[:].rearrange("p (a b) -> p a b", a=8)
                t13 = t1scr[:].rearrange("p (a b) -> p a b", a=8)
                murb = murFb[:].unsqueeze(2).broadcast_to((128, 8, C))
                ins = dv.tensor_tensor(out=d3, in0=xv3, in1=xpv, op=ALU.subtract)
                mark("dv", ins, s_dv, M_dv("mixd", i))
                dv.tensor_tensor(out=t13, in0=d3, in1=murb, op=ALU.mult)
                wt(dv, "dv", s_pe, M_pe("rpre", i))
                ins = dv.tensor_tensor(out=xrT[:].rearrange("p (a b) -> p a b", a=8),
                                       in0=xpv, in1=t13, op=ALU.add)
                mark("dv", ins, s_dv, M_dv("xr", i))
                wt(dv, "dv", s_pe, M_pe("out3", i))
                wt(dv, "dv", s_ac, M_ac("sig", i))
                ins = dv.tensor_tensor(out=y1scr[:], in0=ps_s[:], in1=rs[:],
                                       op=ALU.mult)
                mark("dv", ins, s_dv, M_dv("zmul", i))
            dv.wait_ge(s_pe, M_pe("out3", N - 1) + 4)
            dv.wait_ge(s_ac, M_ac("bcopy", N - 1) + 3)
            wt(dv, "dv", s_gp, M_gp("z", N - 1))
            ins = dv.tensor_tensor(out=y1scr[:], in0=ps_s[:], in1=rs[:],
                                   op=ALU.mult)
            ins.then_inc(s_dv, 1)

        @block.tensor
        def _(pe):
            wt(pe, "pe", s_cst, 16 * N_CONST_DMAS)
            wt(pe, "pe", s_ac, 3)
            for j in range(8):
                ins = pe.matmul(ps_pb2[:, j * 128:(j + 1) * 128],
                                ln1b[0][:, j * 128:(j + 1) * 128], identb[:],
                                start=(j == 0), stop=(j == 7), is_transpose=True)
            ins.then_inc(s_pe, 1)
            emitted["pe"] = PE_PRO
            for i in range(N):
                p = i % 2
                wt(pe, "pe", s_ac, C_lnTev(i))
                wt(pe, "pe", s_ac, M_ac("ln2Tev", i - 1))
                wt(pe, "pe", s_w, 16 * 8)
                for ct in range(8):
                    lh = lnT[p][:, ct * 128:(ct + 1) * 128]
                    st_, sp = (ct == 0), (ct == 7)
                    pe.matmul(ps_p1[:, 0:512], lh,
                              w_s["wqT"][:, ct * D:ct * D + 512], start=st_, stop=sp)
                    ins = pe.matmul(ps_p1[:, 512:1024], lh,
                                    w_s["wqT"][:, ct * D + 512:ct * D + 1024],
                                    start=st_, stop=sp)
                mark("pe", ins, s_pe, M_pe("qg", i))
                wt(pe, "pe", s_w, 16 * 16)
                for ct in range(8):
                    lh = lnT[p][:, ct * 128:(ct + 1) * 128]
                    st_, sp = (ct == 0), (ct == 7)
                    pe.matmul(ps_p2[:, 0:512], lh,
                              w_s["wvT"][:, ct * D:ct * D + 512], start=st_, stop=sp)
                    ins = pe.matmul(ps_p2[:, 512:1024], lh,
                                    w_s["wvT"][:, ct * D + 512:ct * D + 1024],
                                    start=st_, stop=sp)
                mark("pe", ins, s_pe, M_pe("vg", i))
                wt(pe, "pe", s_ac, M_ac("kaTev", i - 1))
                wt(pe, "pe", s_dv, M_dv("qb", i))
                for j in range(8):
                    ins = pe.matmul(ps_pb[:, j * 128:(j + 1) * 128],
                                    qb[p][:, j * 128:(j + 1) * 128], identb[:],
                                    start=(j == 0), stop=(j == 7), is_transpose=True)
                mark("pe", ins, s_pe, M_pe("qT", i))

                def at_quarter(q, dst):
                    for idx in range(4):
                        h = 4 * q + idx
                        lh = lnT[p][:, (h // 2) * 128:(h // 2) * 128 + 128]
                        qz = qTe if h % 2 == 0 else qTo
                        rh = qz[:, (h // 2) * 128:(h // 2) * 128 + 128]
                        ins = pe.matmul(dst[:, idx * 128:(idx + 1) * 128],
                                        lh, rh, start=(idx == 0), stop=(idx == 3))
                    return ins

                def intra_group(hs, dst):
                    for h in hs:
                        ins = pe.matmul(dst[:, (h % 8) * 64:(h % 8) * 64 + 64],
                                        amask[:, h * 128:(h + 1) * 128],
                                        vn[p][:, h * 64:h * 64 + 64],
                                        start=(h in (0, 8)), stop=False)
                    return ins

                wt(pe, "pe", s_dv, M_dv("qTev", i))
                wt(pe, "pe", s_ac, M_ac("sqQ", i))
                ins = at_quarter(0, ps_p1[:, 0:512])
                mark("pe", ins, s_pe, M_pe("at1", i))
                ins = at_quarter(1, ps_p1[:, 512:1024])
                mark("pe", ins, s_pe, M_pe("at2", i))
                wt(pe, "pe", s_dv, M_dv("vn1", i))
                wt(pe, "pe", s_ac, M_ac("sqV", i))
                ins = at_quarter(2, ps_p2[:, 0:512])
                mark("pe", ins, s_pe, M_pe("at3", i))
                ins = at_quarter(3, ps_p2[:, 512:1024])
                mark("pe", ins, s_pe, M_pe("at4", i))
                wt(pe, "pe", s_dv, M_dv("vn2", i))
                wt(pe, "pe", s_dv, M_dv("mke", i))
                intra_group(range(0, 8), ps_p1[:, 0:512])
                wt(pe, "pe", s_dv, M_dv("mko", i))
                ins = intra_group(range(8, 16), ps_p1[:, 512:1024])
                mark("pe", ins, s_pe, M_pe("intra", i))
                wt(pe, "pe", s_gp, M_gp("qtf", i))
                wt(pe, "pe", s_gp, M_gp("sbd", i - 1))
                for ct in range(8):
                    ins = pe.matmul(ps_p1[:, ct * 128:(ct + 1) * 128],
                                    qTf[:, ct * 128:(ct + 1) * 128],
                                    sbd[p][:, ct * 128:(ct + 1) * 128],
                                    start=False, stop=(ct in (3, 7)))
                mark("pe", ins, s_pe, M_pe("inter", i))
                wt(pe, "pe", s_dv, M_dv("sadd", i - 1))
                wt(pe, "pe", s_dv, M_dv("zmul", i - 1))
                for ct in range(8):
                    ins = pe.matmul(ps_s[:, ct * 128:(ct + 1) * 128],
                                    ln1b[p][:, ct * 128:(ct + 1) * 128],
                                    vn[p][:, ct * 128:(ct + 1) * 128],
                                    start=(ct in (0, 4)),
                                    stop=(ct in (3, 7)))
                mark("pe", ins, s_pe, M_pe("g", i))
                wt(pe, "pe", s_gp, M_gp("xkb", i - 1))
                wt(pe, "pe", s_dv, M_dv("mko", i))
                wt(pe, "pe", s_w, 16 * 32)
                for ct in range(8):
                    lh = xkT[:, ct * 128:(ct + 1) * 128]
                    st_, sp = (ct == 0), (ct == 7)
                    pe.matmul(ps_p2[:, 0:512], lh,
                              w_s["wkT"][:, ct * D:ct * D + 512], start=st_, stop=sp)
                    ins = pe.matmul(ps_p2[:, 512:1024], lh,
                                    w_s["wkT"][:, ct * D + 512:ct * D + 1024],
                                    start=st_, stop=sp)
                mark("pe", ins, s_pe, M_pe("kk", i))
                wt(pe, "pe", s_dv, M_dv("xr", i - 1))
                wt(pe, "pe", s_dv, M_dv("y", i))
                wt(pe, "pe", s_w, 16 * 24)
                for ct in range(8):
                    lh = xrT[:, ct * 128:(ct + 1) * 128]
                    st_, sp = (ct == 0), (ct == 7)
                    pe.matmul(ps_p1[:, 0:512], lh,
                              w_s["wrT"][:, ct * D:ct * D + 512], start=st_, stop=sp)
                    ins = pe.matmul(ps_p1[:, 512:1024], lh,
                                    w_s["wrT"][:, ct * D + 512:ct * D + 1024],
                                    start=st_, stop=sp)
                mark("pe", ins, s_pe, M_pe("rpre", i))
                wt(pe, "pe", s_ac, M_ac("kab", i))
                for j in range(8):
                    ins = pe.matmul(ps_pb[:, j * 128:(j + 1) * 128],
                                    kab[:, j * 128:(j + 1) * 128], identb[:],
                                    start=(j == 0), stop=(j == 7), is_transpose=True)
                mark("pe", ins, s_pe, M_pe("kaT", i))
                wt(pe, "pe", s_ac, C_ln1n(i + 1))
                for j in range(8):
                    ins = pe.matmul(ps_pb2[:, j * 128:(j + 1) * 128],
                                    ln1b[(i + 1) % 2][:, j * 128:(j + 1) * 128],
                                    identb[:],
                                    start=(j == 0), stop=(j == 7), is_transpose=True)
                mark("pe", ins, s_pe, M_pe("lnT", i))
                wt(pe, "pe", s_dv, M_dv("nrm2", i))
                wt(pe, "pe", s_ac, M_ac("ybs", i))
                for j in range(8):
                    ins = pe.matmul(ps_p2[:, j * 128:(j + 1) * 128],
                                    ybs[:, j * 128:(j + 1) * 128], diagr[:],
                                    start=(j in (0, 4)), stop=(j in (3, 7)))
                mark("pe", ins, s_pe, M_pe("ln2T", i))
                wt(pe, "pe", s_dv, M_dv("sadd", i))
                wt(pe, "pe", s_ac, M_ac("kaTev", i))
                wt(pe, "pe", s_w, 16 * 40)
                for ct in range(8):
                    lh = kaT[:, ct * 128:(ct + 1) * 128]
                    st_, sp = (ct == 0), (ct == 7)
                    pe.matmul(ps_s[:, 0:512], lh,
                              w_s["woT"][:, ct * D:ct * D + 512], start=st_, stop=sp)
                    ins = pe.matmul(ps_s[:, 512:1024], lh,
                                    w_s["woT"][:, ct * D + 512:ct * D + 1024],
                                    start=st_, stop=sp)
                mark("pe", ins, s_pe, M_pe("out3", i))
            tot_dv = M_dv("zmul", N - 1)
            tot_ac = M_ac("bcopy", N - 1)
            wt(pe, "pe", s_gp, M_gp("xkb", N - 1))
            wt(pe, "pe", s_ac, M_ac("ln2Tev", N - 1))
            for ct in range(8):
                lh = xkT[:, ct * 128:(ct + 1) * 128]
                st_, sp = (ct == 0), (ct == 7)
                pe.matmul(ps_p2[:, 0:512], lh,
                          w_s["wkT"][:, ct * D:ct * D + 512], start=st_, stop=sp)
                ins = pe.matmul(ps_p2[:, 512:1024], lh,
                                w_s["wkT"][:, ct * D + 512:ct * D + 1024],
                                start=st_, stop=sp)
            ins.then_inc(s_pe, 1)
            wt(pe, "pe", s_dv, M_dv("xr", N - 1))
            wt(pe, "pe", s_ac, M_ac("ln2Tev", N - 1))
            for ct in range(8):
                lh = xrT[:, ct * 128:(ct + 1) * 128]
                st_, sp = (ct == 0), (ct == 7)
                pe.matmul(ps_p1[:, 0:512], lh,
                          w_s["wrT"][:, ct * D:ct * D + 512], start=st_, stop=sp)
                ins = pe.matmul(ps_p1[:, 512:1024], lh,
                                w_s["wrT"][:, ct * D + 512:ct * D + 1024],
                                start=st_, stop=sp)
            ins.then_inc(s_pe, 1)
            pe.wait_ge(s_ac, tot_ac + 2)
            for j in range(8):
                ins = pe.matmul(ps_pb[:, j * 128:(j + 1) * 128],
                                kab[:, j * 128:(j + 1) * 128], identb[:],
                                start=(j == 0), stop=(j == 7), is_transpose=True)
            ins.then_inc(s_pe, 1)
            pe.wait_ge(s_ac, tot_ac + 4)
            wt(pe, "pe", s_dv, tot_dv)
            for ct in range(8):
                lh = kaT[:, ct * 128:(ct + 1) * 128]
                st_, sp = (ct == 0), (ct == 7)
                pe.matmul(ps_s[:, 0:512], lh,
                          w_s["woT"][:, ct * D:ct * D + 512], start=st_, stop=sp)
                ins = pe.matmul(ps_s[:, 512:1024], lh,
                                w_s["woT"][:, ct * D + 512:ct * D + 1024],
                                start=st_, stop=sp)
            ins.then_inc(s_pe, 1)

    return nc


def make_host_inputs(Wq, Wv, qw, vw, Wr, Wk, Wo, mur, muk):
    f32 = np.float32
    shared = {
        "wqT": np.ascontiguousarray(np.asarray(Wq, dtype=f32).T),
        "wvT": np.ascontiguousarray(np.asarray(Wv, dtype=f32).T),
        "wrT": np.ascontiguousarray(np.asarray(Wr, dtype=f32).T),
        "wkT": np.ascontiguousarray(np.asarray(Wk, dtype=f32).T),
        "woT": np.ascontiguousarray(np.asarray(Wo, dtype=f32).T),
        "identb": np.eye(128, dtype=bfloat16),
        "qwrep": np.ascontiguousarray(
            np.broadcast_to(np.asarray(qw, dtype=f32).reshape(1, D), (128, D))),
        "vwrep": np.ascontiguousarray(
            np.broadcast_to(np.asarray(vw, dtype=f32).reshape(1, D), (128, D))),
        "maskrep": np.ascontiguousarray(
            np.tile(np.triu(np.ones((128, 128), dtype=f32)), (1, 8))
        ).astype(bfloat16),
        "murF": np.ascontiguousarray(np.asarray(mur, dtype=f32).reshape(8, 128).T),
        "mukF": np.ascontiguousarray(np.asarray(muk, dtype=f32).reshape(8, 128).T),
        "consts": np.ascontiguousarray(np.broadcast_to(
            np.array([1e-5, 1e-8, 1.0 / 64, 1.0], dtype=f32), (128, 4))),
    }
    return shared


_NC_CACHE = {}


def kernel(L0_attn_X, L0_attn_Wq, L0_attn_Wv, L0_attn_q_norm_w, L0_attn_v_norm_w,
           L0_ffn_W_r, L0_ffn_W_k, L0_ffn_W_v, L0_ffn_mu_r, L0_ffn_mu_k):
    X = np.ascontiguousarray(np.asarray(L0_attn_X, dtype=np.float32))
    B, L, _ = X.shape
    n_chunks = L // C
    if n_chunks not in _NC_CACHE:
        _NC_CACHE[n_chunks] = build_program(n_chunks)
    nc = _NC_CACHE[n_chunks]
    shared = make_host_inputs(L0_attn_Wq, L0_attn_Wv, L0_attn_q_norm_w,
                              L0_attn_v_norm_w, L0_ffn_W_r, L0_ffn_W_k,
                              L0_ffn_W_v, L0_ffn_mu_r, L0_ffn_mu_k)
    in_maps = []
    for b in range(B):
        m = {"x": np.ascontiguousarray(X[b])}
        m.update(shared)
        in_maps.append(m)
    res = run_bass_kernel_spmd(nc, in_maps, core_ids=list(range(B)))
    out = np.stack([np.asarray(res.results[b]["z"], dtype=np.float32)
                    for b in range(B)], axis=0)
    return out


# version 31 (final)
# speedup vs baseline: 1.2557x; 1.0196x over previous
import numpy as np
import ml_dtypes
from contextlib import ExitStack

import concourse.bass as bass
import concourse.mybir as mybir
from concourse.bass_utils import run_bass_kernel_spmd

F32 = mybir.dt.float32
BF16 = mybir.dt.bfloat16
AF = mybir.ActivationFunctionType
ALU = mybir.AluOpType
AX = mybir.AxisListType
bfloat16 = ml_dtypes.bfloat16

D = 1024
H = 16
HD = 64
C = 128
N_CORES = 8

PE_LABELS = ["qg", "vg", "qT", "at1", "at2", "at3", "at4",
             "intra", "inter", "g", "kk", "rpre", "kaT", "lnT", "ln2T", "out3"]
DV_LABELS = ["qb", "msq", "qTev", "msv", "vn1", "mke", "vn2",
             "mko", "var1", "nrm1", "sadd", "y", "var2", "nrm2", "mixd", "xr", "zmul"]
AC_LABELS = ["sumsq", "sqQ", "sqV", "sqrtQ", "sqrtV", "sqrt1", "ln1n",
             "relu", "kab", "ysq", "ybs", "sqrt2", "kaTev", "sig",
             "ln2Tev", "lnTev", "bcopy"]
GP_LABELS = ["qtf", "sbd", "xkb", "z"]

NPE, NDV, NAC, NGP = map(len, (PE_LABELS, DV_LABELS, AC_LABELS, GP_LABELS))
PE_PRO = 1
DV_PRO = 3
AC_PRO = 4


def M_pe(lbl, i): return PE_PRO + i * NPE + PE_LABELS.index(lbl) + 1
def M_dv(lbl, i): return DV_PRO + i * NDV + DV_LABELS.index(lbl) + 1
def M_ac(lbl, i): return AC_PRO + i * NAC + AC_LABELS.index(lbl) + 1
def M_gp(lbl, i): return i * NGP + GP_LABELS.index(lbl) + 1


def C_ln1n(i):
    return 3 if i == 0 else M_ac("ln1n", i - 1)


def C_lnTev(i):
    return AC_PRO if i == 0 else M_ac("lnTev", i - 1)


def build_program(n_chunks, dbg=False):
    N = n_chunks
    ROWS = N * C
    nc = bass.Bass("TRN2", target_bir_lowering=False, debug=False,
                   detect_race_conditions=False)

    xd = nc.dram_tensor("x", [ROWS, D], F32, kind="ExternalInput").ap()
    zd = nc.dram_tensor("z", [ROWS, D], F32, kind="ExternalOutput").ap()
    w_d = {n: nc.dram_tensor(n, [D, D], F32, kind="ExternalInput").ap()
           for n in ("wqT", "wvT", "wrT", "wkT", "woT")}
    identb_d = nc.dram_tensor("identb", [128, 128], BF16, kind="ExternalInput").ap()
    qwrep_d = nc.dram_tensor("qwrep", [128, D], F32, kind="ExternalInput").ap()
    vwrep_d = nc.dram_tensor("vwrep", [128, D], F32, kind="ExternalInput").ap()
    mask_d = nc.dram_tensor("maskrep", [128, D], BF16, kind="ExternalInput").ap()
    murF_d = nc.dram_tensor("murF", [128, 8], F32, kind="ExternalInput").ap()
    mukF_d = nc.dram_tensor("mukF", [128, 8], F32, kind="ExternalInput").ap()
    consts_d = nc.dram_tensor("consts", [128, 4], F32, kind="ExternalInput").ap()
    N_CONST_DMAS = 7
    dbg_d = {}

    with ExitStack() as ST:
        def sb(name, shape, dtype=F32):
            return ST.enter_context(nc.sbuf_tensor(name, shape, dtype))

        def psum(name, shape, dtype=F32):
            return ST.enter_context(nc.psum_tensor(name, shape, dtype))

        def sem(name):
            return ST.enter_context(nc.semaphore(name))

        xs = [sb("xs0", [128, D]), sb("xs1", [128, D])]
        zs = sb("zs", [128, D])
        ys = [sb("ys0", [128, D]), sb("ys1", [128, D])]
        ln1b = [sb("ln1b0", [128, D], BF16), sb("ln1b1", [128, D], BF16)]
        lnT = [sb("lnT0", [128, D], BF16), sb("lnT1", [128, D], BF16)]
        qb = [sb("qb0", [128, D], BF16), sb("qb1", [128, D], BF16)]
        qTe = sb("qTe", [128, D], BF16)
        qTo = sb("qTo", [128, D], BF16)
        qTf = sb("qTf", [128, D], BF16)
        vn = [sb("vn0", [128, D], BF16), sb("vn1_", [128, D], BF16)]
        amask = sb("am", [128, 2048], BF16)
        ybs = sb("ybs", [128, D], BF16)
        diagr = sb("diagr", [128, 128], BF16)
        ln2T = [sb("ln2T0", [128, 8 * 129], BF16), sb("ln2T1", [128, 8 * 129], BF16)]
        kaT = sb("kaT", [128, D], BF16)
        sbd = [sb("sbd0", [128, D], BF16), sb("sbd1", [128, D], BF16)]
        Ssum = sb("Ssum", [128, D])
        rs = sb("rs", [128, D])
        kab = sb("kab", [128, D], BF16)
        sqB = sb("sqB", [128, D])
        sqBv = sb("sqBv", [128, D])
        tscr = sb("tscr", [128, D])
        vscr = sb("vscr", [128, D])
        y1scr = sb("y1scr", [128, D])
        w_s = {n: sb(n + "_s", [128, 8 * D], BF16) for n in w_d}
        xrT = sb("xrT", [128, D], BF16)
        xkT = sb("xkT", [128, D], BF16)
        dscr = sb("dscr", [128, D], BF16)
        t1scr = sb("t1scr", [128, D], BF16)
        t1g = sb("t1g", [128, D], BF16)
        murFb = sb("murFb", [128, 8], BF16)
        mukFb = sb("mukFb", [128, 8], BF16)
        identb = sb("identb_s", [128, 128], BF16)
        qwrep = sb("qwrep_s", [128, D])
        vwrep = sb("vwrep_s", [128, D])
        maskrep = sb("maskrep_s", [128, D], BF16)
        murF = sb("murF_s", [128, 8])
        mukF = sb("mukF_s", [128, 8])
        consts = sb("consts_s", [128, 4])
        tin = {n: sb(n, [128, 2]) for n in
               ("s1", "mean1", "m2", "var1", "sqv1", "rstd1", "b1")}
        tin["ss1"] = sb("ss1", [128, 4])
        for a, b in (("s2", "s1"), ("nmean2", "mean1"), ("m22", "m2"),
                     ("var2", "var1"), ("sqv2", "sqv1"), ("rstd2", "rstd1"),
                     ("ss2", "ss1")):
            tin[a] = tin[b]
        hh = {n: sb(n, [128, 32]) for n in
              ("msq", "msv", "sqmsQ", "sqmsV", "rsfq", "rsv")}

        ps_s = psum("ps_s", [128, D])
        ps_p1 = psum("ps_p1", [128, D])
        ps_p2 = psum("ps_p2", [128, D])
        ps_pb = psum("ps_pb", [128, D], BF16)
        ps_pb2 = psum("ps_pb2", [128, D], BF16)

        s_x = [sem("s_x0"), sem("s_x1")]
        s_w = sem("s_w"); s_cst = sem("s_cst")
        s_z = sem("s_z")
        s_pe = sem("s_pe"); s_dv = sem("s_dv"); s_ac = sem("s_ac"); s_gp = sem("s_gp")
        s_fd = sem("s_fd")
        fence_ct = [0]

        def fence(dv_eng, ins):
            fence_ct[0] += 1
            ins.then_inc(s_fd, 1)
            dv_eng.wait_ge(s_fd, fence_ct[0])

        hw = {}

        def wt(eng, key, s, v):
            if v <= 0:
                return
            k = (key, id(s))
            if hw.get(k, 0) >= v:
                return
            hw[k] = v
            eng.wait_ge(s, v)

        emitted = {"pe": 0, "dv": 0, "ac": 0, "gp": 0}

        def mark(key, instr, sem_h, val):
            assert val == emitted[key] + 1, (key, val, emitted[key])
            emitted[key] = val
            instr.then_inc(sem_h, 1)

        def xcount(c):
            return 16 * (c // 2 + 1)

        block = ST.enter_context(nc.Block())

        @block.sync
        def _(sync):
            for ap_s, ap_d in ((identb, identb_d),
                               (qwrep, qwrep_d), (vwrep, vwrep_d),
                               (maskrep, mask_d), (murF, murF_d),
                               (mukF, mukF_d), (consts, consts_d)):
                sync.dma_start(out=ap_s[:], in_=ap_d).then_inc(s_cst, 16)
            sync.dma_start(out=xs[0][:], in_=xd[0:C, :]).then_inc(s_x[0], 16)
            if N > 1:
                sync.dma_start(out=xs[1][:], in_=xd[C:2 * C, :]).then_inc(s_x[1], 16)
            for i in range(N):
                if i + 2 < N:
                    wt(sync, "sy", s_dv, M_dv("y", i))
                    wt(sync, "sy", s_ac, C_ln1n(i))
                    sync.dma_start(out=xs[i % 2][:],
                                   in_=xd[(i + 2) * C:(i + 3) * C, :]).then_inc(s_x[i % 2], 16)
                if i >= 1:
                    wt(sync, "sy", s_gp, M_gp("z", i))
                    sync.dma_start(out=zd[(i - 1) * C:i * C, :],
                                   in_=zs[:]).then_inc(s_z, 16)
            wt(sync, "sy", s_gp, M_gp("z", N - 1) + 1)
            sync.dma_start(out=zd[(N - 1) * C:N * C, :],
                           in_=zs[:]).then_inc(s_z, 16)
            sync.wait_ge(s_z, 16 * N)
            if dbg:
                q_ = (N - 1) % 2
                dbg_map = {"ln1b": ln1b[q_], "lnT": lnT[q_], "qb": qb[q_],
                           "qTe": qTe, "qTo": qTo, "qTf": qTf, "vn": vn[q_],
                           "amask": amask,
                           "ys0": ys[0][:], "ys1": ys[1][:], "Ssum": Ssum,
                           "sbd0": sbd[0][:], "sbd1": sbd[1][:], "rs": rs,
                           "kab": kab, "kaT": kaT, "ybs": ybs,
                           "ln2T0": ln2T[0][:], "ln2T1": ln2T[1][:],
                           "tscr": tscr, "y1scr": y1scr, "sqB": sqB}
                sync.wait_ge(s_dv, M_dv("zmul", N - 1) + 1)
                sync.wait_ge(s_ac, M_ac("bcopy", N - 1) + 4)
                sync.wait_ge(s_gp, M_gp("z", N - 1) + 1)
                nd = 0
                for nm, t in dbg_map.items():
                    ap = t[:] if hasattr(t, "shape") and not isinstance(t, bass.AP) else t
                    dd = nc.dram_tensor("dbg_" + nm, list(ap.shape), ap.dtype,
                                        kind="ExternalOutput").ap()
                    dbg_d[nm] = dd
                    sync.dma_start(out=dd, in_=ap).then_inc(s_cst, 16)
                    nd += 1
                sync.wait_ge(s_cst, 16 * (N_CONST_DMAS + nd))

        @block.gpsimd
        def _(gp):
            for n in ("wqT", "wvT", "wrT", "wkT", "woT"):
                for ct in range(8):
                    gp.dma_start(out=w_s[n][:, ct * D:(ct + 1) * D],
                                 in_=w_d[n][ct * 128:(ct + 1) * 128, :]
                                 ).then_inc(s_w, 16)
            for i in range(N):
                wt(gp, "gp", s_cst, 16 * N_CONST_DMAS)
                wt(gp, "gp", s_dv, 1)
                wt(gp, "gp", s_dv, M_dv("qTev", i))
                wt(gp, "gp", s_pe, M_pe("inter", i - 1))
                ins = gp.tensor_add(qTf[:], qTe[:], qTo[:])
                mark("gp", ins, s_gp, M_gp("qtf", i))
                wt(gp, "gp", s_dv, M_dv("sadd", i))
                sB = Ssum[0:64, :].rearrange("p (ct c) -> p ct c", ct=8)
                dB = sbd[(i + 1) % 2][0:64, :].rearrange("p (ct c) -> p ct c", ct=8)
                gp.tensor_copy(out=dB[:, :, 0:64], in_=sB[:, :, 0:64])
                sB = Ssum[64:128, :].rearrange("p (ct c) -> p ct c", ct=8)
                dB = sbd[(i + 1) % 2][64:128, :].rearrange("p (ct c) -> p ct c", ct=8)
                ins = gp.tensor_copy(out=dB[:, :, 64:128], in_=sB[:, :, 64:128])
                mark("gp", ins, s_gp, M_gp("sbd", i))
                wt(gp, "gp", s_dv, M_dv("mixd", i))
                wt(gp, "gp", s_pe, M_pe("kk", i))
                lt2 = ln2T[i % 2][:].rearrange("p (a b) -> p a b", a=8)
                xpv = lt2[:, :, 0:128]
                d3 = dscr[:].rearrange("p (a b) -> p a b", a=8)
                t13g = t1g[:].rearrange("p (a b) -> p a b", a=8)
                mukb = mukFb[:].unsqueeze(2).broadcast_to((128, 8, C))
                gp.tensor_tensor(out=t13g, in0=d3, in1=mukb, op=ALU.mult)
                ins = gp.tensor_tensor(out=xkT[:].rearrange("p (a b) -> p a b", a=8),
                                       in0=xpv, in1=t13g, op=ALU.add)
                mark("gp", ins, s_gp, M_gp("xkb", i))
                wt(gp, "gp", s_dv, M_dv("zmul", i))
                wt(gp, "gpz", s_z, 16 * (i - 1))
                ins = gp.tensor_tensor(out=zs[:], in0=y1scr[:],
                                       in1=ys[(i - 1) % 2][:], op=ALU.add)
                mark("gp", ins, s_gp, M_gp("z", i))
            gp.wait_ge(s_dv, M_dv("zmul", N - 1) + 1)
            wt(gp, "gpz", s_z, 16 * (N - 1))
            ins = gp.tensor_tensor(out=zs[:], in0=y1scr[:],
                                   in1=ys[(N - 1) % 2][:], op=ALU.add)
            ins.then_inc(s_gp, 1)

        @block.scalar
        def _(ac):
            def ln1_next(c):
                pc = c % 2
                if c < N:
                    wt(ac, "ac%d" % pc, s_x[pc], xcount(c))
                src = xs[pc]
                ac.activation(out=tscr[:, 0:512], in_=src[:, 0:512],
                              func=AF.Square,
                              accum_out=tin["ss1"][:, 2 * pc:2 * pc + 1])
                ins = ac.activation(out=tscr[:, 512:1024], in_=src[:, 512:1024],
                                    func=AF.Square,
                                    accum_out=tin["ss1"][:, 2 * pc + 1:2 * pc + 2])
                yield ins
                ins = ac.activation(out=tin["sqv1"][:, pc:pc + 1],
                                    in_=tin["var1"][:, pc:pc + 1], func=AF.Sqrt,
                                    bias=consts[:, 0:1])
                yield ins
                ins = ac.activation(out=ln1b[pc][:], in_=src[:], func=AF.Identity,
                                    bias=tin["b1"][:, pc:pc + 1],
                                    scale=tin["rstd1"][:, pc:pc + 1])
                yield ins

            wt(ac, "ac", s_cst, 16 * N_CONST_DMAS)
            g0 = ln1_next(0)
            ins = next(g0)
            ins.then_inc(s_ac, 1)
            wt(ac, "ac", s_dv, 2)
            ins = next(g0)
            ins.then_inc(s_ac, 2 - 1)
            wt(ac, "ac", s_dv, 3)
            ins = next(g0)
            ins.then_inc(s_ac, 1)
            wt(ac, "ac", s_pe, PE_PRO)
            ins = ac.copy(out=lnT[0][:], in_=ps_pb2[:])
            ins.then_inc(s_ac, 1)
            emitted["ac"] = AC_PRO

            for i in range(N):
                p = i % 2
                g1 = ln1_next(i + 1)
                ins = next(g1)
                mark("ac", ins, s_ac, M_ac("sumsq", i))
                wt(ac, "ac", s_pe, M_pe("qg", i))
                ins = ac.activation(out=sqB[:], in_=ps_p1[:], func=AF.Square)
                mark("ac", ins, s_ac, M_ac("sqQ", i))
                wt(ac, "ac", s_pe, M_pe("vg", i))
                ins = ac.activation(out=sqBv[:], in_=ps_p2[:], func=AF.Square)
                mark("ac", ins, s_ac, M_ac("sqV", i))
                wt(ac, "ac", s_dv, M_dv("msq", i))
                ins = ac.activation(out=hh["sqmsQ"][:, p * 16:(p + 1) * 16],
                                    in_=hh["msq"][:, p * 16:(p + 1) * 16],
                                    func=AF.Sqrt, bias=consts[:, 1:2],
                                    scale=consts[:, 2:3])
                mark("ac", ins, s_ac, M_ac("sqrtQ", i))
                wt(ac, "ac", s_dv, M_dv("msv", i))
                ins = ac.activation(out=hh["sqmsV"][:, p * 16:(p + 1) * 16],
                                    in_=hh["msv"][:, p * 16:(p + 1) * 16],
                                    func=AF.Sqrt, bias=consts[:, 1:2],
                                    scale=consts[:, 2:3])
                mark("ac", ins, s_ac, M_ac("sqrtV", i))
                wt(ac, "ac", s_dv, M_dv("var1", i))
                ins = next(g1)
                mark("ac", ins, s_ac, M_ac("sqrt1", i))
                wt(ac, "ac", s_dv, M_dv("nrm1", i))
                wt(ac, "ac", s_pe, M_pe("g", i - 1))
                ins = next(g1)
                mark("ac", ins, s_ac, M_ac("ln1n", i))
                wt(ac, "ac", s_pe, M_pe("kk", i))
                ins = ac.activation(out=tscr[:], in_=ps_p2[:], func=AF.Relu)
                mark("ac", ins, s_ac, M_ac("relu", i))
                wt(ac, "ac", s_pe, M_pe("kaT", i - 1))
                ins = ac.activation(out=kab[:], in_=tscr[:], func=AF.Square)
                mark("ac", ins, s_ac, M_ac("kab", i))
                wt(ac, "ac", s_dv, M_dv("y", i))
                ac.activation(out=tscr[:, 0:512], in_=ys[p][:, 0:512],
                              func=AF.Square,
                              accum_out=tin["ss2"][:, 2 * p:2 * p + 1])
                ins = ac.activation(out=tscr[:, 512:1024], in_=ys[p][:, 512:1024],
                                    func=AF.Square,
                                    accum_out=tin["ss2"][:, 2 * p + 1:2 * p + 2])
                mark("ac", ins, s_ac, M_ac("ysq", i))
                wt(ac, "ac", s_pe, M_pe("ln2T", i - 1))
                ins = ac.activation(out=ybs[:], in_=ys[p][:], func=AF.Identity,
                                    bias=tin["nmean2"][:, p:p + 1])
                mark("ac", ins, s_ac, M_ac("ybs", i))
                wt(ac, "ac", s_dv, M_dv("var2", i))
                ins = ac.activation(out=tin["sqv2"][:, p:p + 1],
                                    in_=tin["var2"][:, p:p + 1], func=AF.Sqrt,
                                    bias=consts[:, 0:1])
                mark("ac", ins, s_ac, M_ac("sqrt2", i))
                wt(ac, "ac", s_pe, M_pe("kaT", i))
                wt(ac, "ac", s_pe, M_pe("out3", i - 1))
                ins = ac.copy(out=kaT[:], in_=ps_pb[:])
                mark("ac", ins, s_ac, M_ac("kaTev", i))
                wt(ac, "ac", s_pe, M_pe("rpre", i))
                wt(ac, "ac", s_dv, M_dv("zmul", i - 1))
                ins = ac.activation(out=rs[:], in_=ps_p1[:], func=AF.Sigmoid)
                mark("ac", ins, s_ac, M_ac("sig", i))
                wt(ac, "ac", s_pe, M_pe("ln2T", i))
                wt(ac, "ac", s_dv, M_dv("xr", i - 2))
                wt(ac, "ac", s_gp, M_gp("xkb", i - 2))
                dst = ln2T[p][:].rearrange("p (a b) -> p a b", a=8)[:, :, 1:129]
                ins = ac.copy(out=dst, in_=ps_p2[:].rearrange("p (a b) -> p a b", a=8))
                mark("ac", ins, s_ac, M_ac("ln2Tev", i))
                wt(ac, "ac", s_pe, M_pe("lnT", i))
                wt(ac, "ac", s_pe, M_pe("at4", i - 1))
                ins = ac.copy(out=lnT[(i + 1) % 2][:], in_=ps_pb2[:])
                mark("ac", ins, s_ac, M_ac("lnTev", i))
                wt(ac, "ac", s_dv, M_dv("xr", i - 1))
                wt(ac, "ac", s_gp, M_gp("xkb", i - 1))
                src = ln2T[p][:].rearrange("p (a b) -> p a b", a=8)[:, :, 128:129]
                dst = ln2T[(i + 1) % 2][:].rearrange("p (a b) -> p a b", a=8)[:, :, 0:1]
                ins = ac.copy(out=dst, in_=src)
                mark("ac", ins, s_ac, M_ac("bcopy", i))
            tot_pe = M_pe("out3", N - 1)
            tot_dv = M_dv("zmul", N - 1)
            ac.wait_ge(s_pe, tot_pe + 1)
            ac.activation(out=tscr[:], in_=ps_p2[:],
                          func=AF.Relu).then_inc(s_ac, 1)
            ac.activation(out=kab[:], in_=tscr[:],
                          func=AF.Square).then_inc(s_ac, 1)
            ac.wait_ge(s_pe, tot_pe + 2)
            ac.wait_ge(s_dv, tot_dv)
            ac.activation(out=rs[:], in_=ps_p1[:],
                          func=AF.Sigmoid).then_inc(s_ac, 1)
            ac.wait_ge(s_pe, tot_pe + 3)
            ac.copy(out=kaT[:], in_=ps_pb[:]).then_inc(s_ac, 1)

        @block.vector
        def _(dv):

            def ln1_stats(c):
                pc = c % 2
                if c < N:
                    wt(dv, "dv%d" % pc, s_x[pc], xcount(c))
                ins = dv.tensor_reduce(out=tin["s1"][:, pc:pc + 1], in_=xs[pc][:],
                                       axis=AX.X, op=ALU.add)
                fence(dv, ins)
                ins = dv.tensor_single_scalar(out=tin["mean1"][:, pc:pc + 1],
                                              in_=tin["s1"][:, pc:pc + 1],
                                              scalar=1.0 / D, op=ALU.mult)
                fence(dv, ins)
                dv.tensor_tensor(out=tin["m2"][:, pc:pc + 1],
                                 in0=tin["mean1"][:, pc:pc + 1],
                                 in1=tin["mean1"][:, pc:pc + 1], op=ALU.mult)
                ins = dv.tensor_add(tin["ss1"][:, 2 * pc:2 * pc + 1],
                                    tin["ss1"][:, 2 * pc:2 * pc + 1],
                                    tin["ss1"][:, 2 * pc + 1:2 * pc + 2])
                fence(dv, ins)
                ins = dv.scalar_tensor_tensor(out=tin["var1"][:, pc:pc + 1],
                                              in0=tin["ss1"][:, 2 * pc:2 * pc + 1],
                                              scalar=1.0 / D,
                                              in1=tin["m2"][:, pc:pc + 1],
                                              op0=ALU.mult, op1=ALU.subtract)
                yield ins
                ins = dv.reciprocal(out=tin["rstd1"][:, pc:pc + 1],
                                    in_=tin["sqv1"][:, pc:pc + 1])
                fence(dv, ins)
                ins = dv.scalar_tensor_tensor(out=tin["b1"][:, pc:pc + 1],
                                              in0=tin["mean1"][:, pc:pc + 1],
                                              scalar=-1.0,
                                              in1=tin["rstd1"][:, pc:pc + 1],
                                              op0=ALU.mult, op1=ALU.mult)
                yield ins

            dv.memset(sbd[0][:], 0.0)
            dv.memset(sbd[1][:], 0.0)
            dv.memset(Ssum[:], 0.0)
            dv.memset(qTe[:], 0.0)
            dv.memset(qTo[:], 0.0)
            dv.memset(ys[1][:], 0.0)
            dv.memset(xrT[:], 0.0)
            dv.memset(xkT[:], 0.0)
            wt(dv, "dv", s_cst, 16 * N_CONST_DMAS)
            ins = dv.memset(
                ln2T[0][:].rearrange("p (a b) -> p a b", a=8)[:, :, 0:1], 0.0)
            ins.then_inc(s_dv, 1)
            g0 = ln1_stats(0)
            wt(dv, "dv", s_ac, 1)
            ins = next(g0)
            ins.then_inc(s_dv, 1)
            wt(dv, "dv", s_ac, 2)
            ins = next(g0)
            ins.then_inc(s_dv, 1)
            dv.tensor_copy(out=murFb[:], in_=murF[:])
            dv.tensor_copy(out=mukFb[:], in_=mukF[:])
            emitted["dv"] = DV_PRO

            for i in range(N):
                p = i % 2
                g1 = ln1_stats(i + 1)
                wt(dv, "dv", s_ac, M_ac("sqQ", i))
                wt(dv, "dv", s_pe, M_pe("qT", i - 2))
                ins = dv.tensor_tensor(out=qb[p][:], in0=ps_p1[:], in1=qwrep[:],
                                       op=ALU.mult)
                mark("dv", ins, s_dv, M_dv("qb", i))
                ins = dv.tensor_reduce(out=hh["msq"][:, p * 16:(p + 1) * 16],
                                       in_=sqB[:].rearrange("p (h e) -> p h e", h=16),
                                       axis=AX.X, op=ALU.add)
                mark("dv", ins, s_dv, M_dv("msq", i))
                wt(dv, "dv", s_pe, M_pe("qT", i))
                wt(dv, "dv", s_pe, M_pe("at4", i - 1))
                wt(dv, "dv", s_gp, M_gp("qtf", i - 1))
                dv.tensor_copy(out=qTe[0:64, :], in_=ps_pb[0:64, :])
                ins = dv.tensor_copy(out=qTo[64:128, :], in_=ps_pb[64:128, :])
                mark("dv", ins, s_dv, M_dv("qTev", i))
                wt(dv, "dv", s_ac, M_ac("sqV", i))
                ins = dv.tensor_reduce(out=hh["msv"][:, p * 16:(p + 1) * 16],
                                       in_=sqBv[:].rearrange("p (h e) -> p h e", h=16),
                                       axis=AX.X, op=ALU.add)
                mark("dv", ins, s_dv, M_dv("msv", i))
                wt(dv, "dv", s_pe, M_pe("vg", i))
                ins = dv.tensor_tensor(out=vscr[:], in0=ps_p2[:], in1=vwrep[:],
                                       op=ALU.mult)
                mark("dv", ins, s_dv, M_dv("vn1", i))
                wt(dv, "dv", s_pe, M_pe("intra", i - 1))
                wt(dv, "dv", s_pe, M_pe("at2", i))
                ins = dv.tensor_tensor(out=amask[:, 0:1024], in0=ps_p1[:],
                                       in1=maskrep[:], op=ALU.mult)
                mark("dv", ins, s_dv, M_dv("mke", i))
                wt(dv, "dv", s_ac, M_ac("sqrtV", i))
                ins = dv.reciprocal(out=hh["rsv"][:, p * 16:(p + 1) * 16],
                                    in_=hh["sqmsV"][:, p * 16:(p + 1) * 16])
                fence(dv, ins)
                wt(dv, "dv", s_pe, M_pe("g", i - 2))
                rsvb = hh["rsv"][:, p * 16:(p + 1) * 16].unsqueeze(2) \
                    .broadcast_to((128, 16, 64))
                ins = dv.tensor_tensor(out=vn[p][:].rearrange("p (h e) -> p h e", h=16),
                                       in0=vscr[:].rearrange("p (h e) -> p h e", h=16),
                                       in1=rsvb, op=ALU.mult)
                mark("dv", ins, s_dv, M_dv("vn2", i))
                wt(dv, "dv", s_pe, M_pe("at4", i))
                ins = dv.tensor_tensor(out=amask[:, 1024:2048], in0=ps_p2[:],
                                       in1=maskrep[:], op=ALU.mult)
                mark("dv", ins, s_dv, M_dv("mko", i))
                wt(dv, "dv", s_ac, M_ac("sumsq", i))
                ins = next(g1)
                mark("dv", ins, s_dv, M_dv("var1", i))
                wt(dv, "dv", s_ac, M_ac("sqrt1", i))
                ins = next(g1)
                mark("dv", ins, s_dv, M_dv("nrm1", i))
                wt(dv, "dv", s_pe, M_pe("g", i))
                wt(dv, "dv", s_gp, M_gp("sbd", i - 1))
                ins = dv.tensor_tensor(out=Ssum[:], in0=Ssum[:], in1=ps_s[:],
                                       op=ALU.add)
                mark("dv", ins, s_dv, M_dv("sadd", i))
                wt(dv, "dv", s_ac, M_ac("sqrtQ", i))
                ins = dv.reciprocal(out=hh["rsfq"][:, p * 16:(p + 1) * 16],
                                    in_=hh["sqmsQ"][:, p * 16:(p + 1) * 16])
                fence(dv, ins)
                wt(dv, "dv", s_pe, M_pe("inter", i))
                wt(dv, "dv", s_gp, M_gp("z", i - 1))
                rsfb = hh["rsfq"][:, p * 16:(p + 1) * 16].unsqueeze(2) \
                    .broadcast_to((128, 16, 64))
                dv.tensor_tensor(out=y1scr[:].rearrange("p (h e) -> p h e", h=16),
                                 in0=ps_p1[:].rearrange("p (h e) -> p h e", h=16),
                                 in1=rsfb, op=ALU.mult)
                dv.scalar_tensor_tensor(out=ys[p][:], in0=y1scr[:],
                                        scalar=1.0, in1=xs[p][:],
                                        op0=ALU.mult, op1=ALU.add,
                                        accum_out=tin["s2"][:, p:p + 1])
                ins = dv.tensor_single_scalar(out=tin["nmean2"][:, p:p + 1],
                                              in_=tin["s2"][:, p:p + 1],
                                              scalar=-1.0 / D, op=ALU.mult)
                mark("dv", ins, s_dv, M_dv("y", i))
                ins = dv.scalar_tensor_tensor(out=tin["m22"][:, p:p + 1],
                                              in0=tin["s2"][:, p:p + 1],
                                              scalar=1.0 / (D * D),
                                              in1=tin["s2"][:, p:p + 1],
                                              op0=ALU.mult, op1=ALU.mult)
                fence(dv, ins)
                wt(dv, "dv", s_ac, M_ac("ysq", i))
                ins = dv.tensor_add(tin["ss2"][:, 2 * p:2 * p + 1],
                                    tin["ss2"][:, 2 * p:2 * p + 1],
                                    tin["ss2"][:, 2 * p + 1:2 * p + 2])
                fence(dv, ins)
                ins = dv.scalar_tensor_tensor(out=tin["var2"][:, p:p + 1],
                                              in0=tin["ss2"][:, 2 * p:2 * p + 1],
                                              scalar=1.0 / D,
                                              in1=tin["m22"][:, p:p + 1],
                                              op0=ALU.mult, op1=ALU.subtract)
                mark("dv", ins, s_dv, M_dv("var2", i))
                wt(dv, "dv", s_ac, M_ac("sqrt2", i))
                ins = dv.reciprocal(out=tin["rstd2"][:, p:p + 1],
                                    in_=tin["sqv2"][:, p:p + 1])
                fence(dv, ins)
                ins = dv.tensor_single_scalar(out=diagr[:], in_=identb[:],
                                              scalar=tin["rstd2"][:, p:p + 1],
                                              op=ALU.mult)
                mark("dv", ins, s_dv, M_dv("nrm2", i))
                wt(dv, "dv", s_ac, M_ac("ln2Tev", i))
                wt(dv, "dv", s_gp, M_gp("xkb", i - 1))
                lt2 = ln2T[p][:].rearrange("p (a b) -> p a b", a=8)
                xv3 = lt2[:, :, 1:129]
                xpv = lt2[:, :, 0:128]
                d3 = dscr[:].rearrange("p (a b) -> p a b", a=8)
                t13 = t1scr[:].rearrange("p (a b) -> p a b", a=8)
                murb = murFb[:].unsqueeze(2).broadcast_to((128, 8, C))
                ins = dv.tensor_tensor(out=d3, in0=xv3, in1=xpv, op=ALU.subtract)
                mark("dv", ins, s_dv, M_dv("mixd", i))
                dv.tensor_tensor(out=t13, in0=d3, in1=murb, op=ALU.mult)
                wt(dv, "dv", s_pe, M_pe("rpre", i))
                ins = dv.tensor_tensor(out=xrT[:].rearrange("p (a b) -> p a b", a=8),
                                       in0=xpv, in1=t13, op=ALU.add)
                mark("dv", ins, s_dv, M_dv("xr", i))
                wt(dv, "dv", s_pe, M_pe("out3", i))
                wt(dv, "dv", s_ac, M_ac("sig", i))
                ins = dv.tensor_tensor(out=y1scr[:], in0=ps_s[:], in1=rs[:],
                                       op=ALU.mult)
                mark("dv", ins, s_dv, M_dv("zmul", i))
            dv.wait_ge(s_pe, M_pe("out3", N - 1) + 4)
            dv.wait_ge(s_ac, M_ac("bcopy", N - 1) + 3)
            wt(dv, "dv", s_gp, M_gp("z", N - 1))
            ins = dv.tensor_tensor(out=y1scr[:], in0=ps_s[:], in1=rs[:],
                                   op=ALU.mult)
            ins.then_inc(s_dv, 1)

        @block.tensor
        def _(pe):
            wt(pe, "pe", s_cst, 16 * N_CONST_DMAS)
            wt(pe, "pe", s_ac, 3)
            for j in range(8):
                ins = pe.matmul(ps_pb2[:, j * 128:(j + 1) * 128],
                                ln1b[0][:, j * 128:(j + 1) * 128], identb[:],
                                start=(j == 0), stop=(j == 7), is_transpose=True)
            ins.then_inc(s_pe, 1)
            emitted["pe"] = PE_PRO
            for i in range(N):
                p = i % 2
                wt(pe, "pe", s_ac, C_lnTev(i))
                wt(pe, "pe", s_ac, M_ac("ln2Tev", i - 1))
                wt(pe, "pe", s_w, 16 * 8)
                for ct in range(8):
                    lh = lnT[p][:, ct * 128:(ct + 1) * 128]
                    st_, sp = (ct == 0), (ct == 7)
                    pe.matmul(ps_p1[:, 0:512], lh,
                              w_s["wqT"][:, ct * D:ct * D + 512], start=st_, stop=sp)
                    ins = pe.matmul(ps_p1[:, 512:1024], lh,
                                    w_s["wqT"][:, ct * D + 512:ct * D + 1024],
                                    start=st_, stop=sp)
                mark("pe", ins, s_pe, M_pe("qg", i))
                wt(pe, "pe", s_w, 16 * 16)
                for ct in range(8):
                    lh = lnT[p][:, ct * 128:(ct + 1) * 128]
                    st_, sp = (ct == 0), (ct == 7)
                    pe.matmul(ps_p2[:, 0:512], lh,
                              w_s["wvT"][:, ct * D:ct * D + 512], start=st_, stop=sp)
                    ins = pe.matmul(ps_p2[:, 512:1024], lh,
                                    w_s["wvT"][:, ct * D + 512:ct * D + 1024],
                                    start=st_, stop=sp)
                mark("pe", ins, s_pe, M_pe("vg", i))
                wt(pe, "pe", s_ac, M_ac("kaTev", i - 1))
                wt(pe, "pe", s_dv, M_dv("qb", i))
                for j in range(8):
                    ins = pe.matmul(ps_pb[:, j * 128:(j + 1) * 128],
                                    qb[p][:, j * 128:(j + 1) * 128], identb[:],
                                    start=(j == 0), stop=(j == 7), is_transpose=True)
                mark("pe", ins, s_pe, M_pe("qT", i))

                def at_quarter(q, dst):
                    for idx in range(4):
                        h = 4 * q + idx
                        lh = lnT[p][:, (h // 2) * 128:(h // 2) * 128 + 128]
                        qz = qTe if h % 2 == 0 else qTo
                        rh = qz[:, (h // 2) * 128:(h // 2) * 128 + 128]
                        ins = pe.matmul(dst[:, idx * 128:(idx + 1) * 128],
                                        lh, rh, start=(idx == 0), stop=(idx == 3))
                    return ins

                def intra_group(hs, dst):
                    for h in hs:
                        ins = pe.matmul(dst[:, (h % 8) * 64:(h % 8) * 64 + 64],
                                        amask[:, h * 128:(h + 1) * 128],
                                        vn[p][:, h * 64:h * 64 + 64],
                                        start=(h in (0, 8)), stop=False)
                    return ins

                wt(pe, "pe", s_dv, M_dv("qTev", i))
                wt(pe, "pe", s_ac, M_ac("sqQ", i))
                ins = at_quarter(0, ps_p1[:, 0:512])
                mark("pe", ins, s_pe, M_pe("at1", i))
                ins = at_quarter(1, ps_p1[:, 512:1024])
                mark("pe", ins, s_pe, M_pe("at2", i))
                wt(pe, "pe", s_dv, M_dv("vn1", i))
                wt(pe, "pe", s_ac, M_ac("sqV", i))
                ins = at_quarter(2, ps_p2[:, 0:512])
                mark("pe", ins, s_pe, M_pe("at3", i))
                ins = at_quarter(3, ps_p2[:, 512:1024])
                mark("pe", ins, s_pe, M_pe("at4", i))
                wt(pe, "pe", s_dv, M_dv("vn2", i))
                wt(pe, "pe", s_dv, M_dv("mke", i))
                intra_group(range(0, 8), ps_p1[:, 0:512])
                wt(pe, "pe", s_dv, M_dv("mko", i))
                ins = intra_group(range(8, 16), ps_p1[:, 512:1024])
                mark("pe", ins, s_pe, M_pe("intra", i))
                wt(pe, "pe", s_gp, M_gp("qtf", i))
                wt(pe, "pe", s_gp, M_gp("sbd", i - 1))
                for ct in range(8):
                    ins = pe.matmul(ps_p1[:, ct * 128:(ct + 1) * 128],
                                    qTf[:, ct * 128:(ct + 1) * 128],
                                    sbd[p][:, ct * 128:(ct + 1) * 128],
                                    start=False, stop=(ct in (3, 7)))
                mark("pe", ins, s_pe, M_pe("inter", i))
                wt(pe, "pe", s_dv, M_dv("sadd", i - 1))
                wt(pe, "pe", s_dv, M_dv("zmul", i - 1))
                for ct in range(8):
                    ins = pe.matmul(ps_s[:, ct * 128:(ct + 1) * 128],
                                    ln1b[p][:, ct * 128:(ct + 1) * 128],
                                    vn[p][:, ct * 128:(ct + 1) * 128],
                                    start=(ct in (0, 4)),
                                    stop=(ct in (3, 7)))
                mark("pe", ins, s_pe, M_pe("g", i))
                wt(pe, "pe", s_gp, M_gp("xkb", i - 1))
                wt(pe, "pe", s_dv, M_dv("mko", i))
                wt(pe, "pe", s_w, 16 * 32)
                for ct in range(8):
                    lh = xkT[:, ct * 128:(ct + 1) * 128]
                    st_, sp = (ct == 0), (ct == 7)
                    pe.matmul(ps_p2[:, 0:512], lh,
                              w_s["wkT"][:, ct * D:ct * D + 512], start=st_, stop=sp)
                    ins = pe.matmul(ps_p2[:, 512:1024], lh,
                                    w_s["wkT"][:, ct * D + 512:ct * D + 1024],
                                    start=st_, stop=sp)
                mark("pe", ins, s_pe, M_pe("kk", i))
                wt(pe, "pe", s_dv, M_dv("xr", i - 1))
                wt(pe, "pe", s_dv, M_dv("y", i))
                wt(pe, "pe", s_w, 16 * 24)
                for ct in range(8):
                    lh = xrT[:, ct * 128:(ct + 1) * 128]
                    st_, sp = (ct == 0), (ct == 7)
                    pe.matmul(ps_p1[:, 0:512], lh,
                              w_s["wrT"][:, ct * D:ct * D + 512], start=st_, stop=sp)
                    ins = pe.matmul(ps_p1[:, 512:1024], lh,
                                    w_s["wrT"][:, ct * D + 512:ct * D + 1024],
                                    start=st_, stop=sp)
                mark("pe", ins, s_pe, M_pe("rpre", i))
                wt(pe, "pe", s_ac, M_ac("kab", i))
                for j in range(8):
                    ins = pe.matmul(ps_pb[:, j * 128:(j + 1) * 128],
                                    kab[:, j * 128:(j + 1) * 128], identb[:],
                                    start=(j == 0), stop=(j == 7), is_transpose=True)
                mark("pe", ins, s_pe, M_pe("kaT", i))
                wt(pe, "pe", s_ac, C_ln1n(i + 1))
                for j in range(8):
                    ins = pe.matmul(ps_pb2[:, j * 128:(j + 1) * 128],
                                    ln1b[(i + 1) % 2][:, j * 128:(j + 1) * 128],
                                    identb[:],
                                    start=(j == 0), stop=(j == 7), is_transpose=True)
                mark("pe", ins, s_pe, M_pe("lnT", i))
                wt(pe, "pe", s_dv, M_dv("nrm2", i))
                wt(pe, "pe", s_ac, M_ac("ybs", i))
                for j in range(8):
                    ins = pe.matmul(ps_p2[:, j * 128:(j + 1) * 128],
                                    ybs[:, j * 128:(j + 1) * 128], diagr[:],
                                    start=(j in (0, 4)), stop=(j in (3, 7)))
                mark("pe", ins, s_pe, M_pe("ln2T", i))
                wt(pe, "pe", s_dv, M_dv("sadd", i))
                wt(pe, "pe", s_ac, M_ac("kaTev", i))
                wt(pe, "pe", s_w, 16 * 40)
                for ct in range(8):
                    lh = kaT[:, ct * 128:(ct + 1) * 128]
                    st_, sp = (ct == 0), (ct == 7)
                    pe.matmul(ps_s[:, 0:512], lh,
                              w_s["woT"][:, ct * D:ct * D + 512], start=st_, stop=sp)
                    ins = pe.matmul(ps_s[:, 512:1024], lh,
                                    w_s["woT"][:, ct * D + 512:ct * D + 1024],
                                    start=st_, stop=sp)
                mark("pe", ins, s_pe, M_pe("out3", i))
            tot_dv = M_dv("zmul", N - 1)
            tot_ac = M_ac("bcopy", N - 1)
            wt(pe, "pe", s_gp, M_gp("xkb", N - 1))
            wt(pe, "pe", s_ac, M_ac("ln2Tev", N - 1))
            for ct in range(8):
                lh = xkT[:, ct * 128:(ct + 1) * 128]
                st_, sp = (ct == 0), (ct == 7)
                pe.matmul(ps_p2[:, 0:512], lh,
                          w_s["wkT"][:, ct * D:ct * D + 512], start=st_, stop=sp)
                ins = pe.matmul(ps_p2[:, 512:1024], lh,
                                w_s["wkT"][:, ct * D + 512:ct * D + 1024],
                                start=st_, stop=sp)
            ins.then_inc(s_pe, 1)
            wt(pe, "pe", s_dv, M_dv("xr", N - 1))
            wt(pe, "pe", s_ac, M_ac("ln2Tev", N - 1))
            for ct in range(8):
                lh = xrT[:, ct * 128:(ct + 1) * 128]
                st_, sp = (ct == 0), (ct == 7)
                pe.matmul(ps_p1[:, 0:512], lh,
                          w_s["wrT"][:, ct * D:ct * D + 512], start=st_, stop=sp)
                ins = pe.matmul(ps_p1[:, 512:1024], lh,
                                w_s["wrT"][:, ct * D + 512:ct * D + 1024],
                                start=st_, stop=sp)
            ins.then_inc(s_pe, 1)
            pe.wait_ge(s_ac, tot_ac + 2)
            for j in range(8):
                ins = pe.matmul(ps_pb[:, j * 128:(j + 1) * 128],
                                kab[:, j * 128:(j + 1) * 128], identb[:],
                                start=(j == 0), stop=(j == 7), is_transpose=True)
            ins.then_inc(s_pe, 1)
            pe.wait_ge(s_ac, tot_ac + 4)
            wt(pe, "pe", s_dv, tot_dv)
            for ct in range(8):
                lh = kaT[:, ct * 128:(ct + 1) * 128]
                st_, sp = (ct == 0), (ct == 7)
                pe.matmul(ps_s[:, 0:512], lh,
                          w_s["woT"][:, ct * D:ct * D + 512], start=st_, stop=sp)
                ins = pe.matmul(ps_s[:, 512:1024], lh,
                                w_s["woT"][:, ct * D + 512:ct * D + 1024],
                                start=st_, stop=sp)
            ins.then_inc(s_pe, 1)

    return nc


def make_host_inputs(Wq, Wv, qw, vw, Wr, Wk, Wo, mur, muk):
    f32 = np.float32
    shared = {
        "wqT": np.ascontiguousarray(np.asarray(Wq, dtype=f32).T),
        "wvT": np.ascontiguousarray(np.asarray(Wv, dtype=f32).T),
        "wrT": np.ascontiguousarray(np.asarray(Wr, dtype=f32).T),
        "wkT": np.ascontiguousarray(np.asarray(Wk, dtype=f32).T),
        "woT": np.ascontiguousarray(np.asarray(Wo, dtype=f32).T),
        "identb": np.eye(128, dtype=bfloat16),
        "qwrep": np.ascontiguousarray(
            np.broadcast_to(np.asarray(qw, dtype=f32).reshape(1, D), (128, D))),
        "vwrep": np.ascontiguousarray(
            np.broadcast_to(np.asarray(vw, dtype=f32).reshape(1, D), (128, D))),
        "maskrep": np.ascontiguousarray(
            np.tile(np.triu(np.ones((128, 128), dtype=f32)), (1, 8))
        ).astype(bfloat16),
        "murF": np.ascontiguousarray(np.asarray(mur, dtype=f32).reshape(8, 128).T),
        "mukF": np.ascontiguousarray(np.asarray(muk, dtype=f32).reshape(8, 128).T),
        "consts": np.ascontiguousarray(np.broadcast_to(
            np.array([1e-5, 1e-8, 1.0 / 64, 1.0], dtype=f32), (128, 4))),
    }
    return shared


_NC_CACHE = {}


def kernel(L0_attn_X, L0_attn_Wq, L0_attn_Wv, L0_attn_q_norm_w, L0_attn_v_norm_w,
           L0_ffn_W_r, L0_ffn_W_k, L0_ffn_W_v, L0_ffn_mu_r, L0_ffn_mu_k):
    X = np.ascontiguousarray(np.asarray(L0_attn_X, dtype=np.float32))
    B, L, _ = X.shape
    n_chunks = L // C
    if n_chunks not in _NC_CACHE:
        _NC_CACHE[n_chunks] = build_program(n_chunks)
    nc = _NC_CACHE[n_chunks]
    shared = make_host_inputs(L0_attn_Wq, L0_attn_Wv, L0_attn_q_norm_w,
                              L0_attn_v_norm_w, L0_ffn_W_r, L0_ffn_W_k,
                              L0_ffn_W_v, L0_ffn_mu_r, L0_ffn_mu_k)
    in_maps = []
    for b in range(B):
        m = {"x": np.ascontiguousarray(X[b])}
        m.update(shared)
        in_maps.append(m)
    res = run_bass_kernel_spmd(nc, in_maps, core_ids=list(range(B)))
    out = np.stack([np.asarray(res.results[b]["z"], dtype=np.float32)
                    for b in range(B)], axis=0)
    return out
